# version 01 (final)
# baseline (speedup 1.0000x reference)
import numpy as np
import ml_dtypes
import contextlib
import concourse.bass as bass
import concourse.mybir as mybir

F32 = mybir.dt.float32
BF16 = mybir.dt.bfloat16
AF = mybir.ActivationFunctionType
ALU = mybir.AluOpType
AX = mybir.AxisListType

D = 1024; H = 16; KC = 8
I = 4096; E = 8; NIT = 32
TCORE = 4096; NTT = 4; TT = 1024
EPS = 1.0e-5; EPSA = 1.0e-6


class Ev:
    __slots__ = ("sem", "step", "n")
    def __init__(self, sem, step=1):
        self.sem = sem; self.step = step; self.n = 0
    def mark(self):
        return self.n


class Rec:
    def __init__(self, name):
        self.name = name; self.thunks = []; self.dma_ev = None
    def wait(self, ev, cnt=None):
        c = ev.n if cnt is None else cnt
        if c > 0:
            self.thunks.append(("w", ev.sem, c))
    def op(self, meth, *args, inc=None, **kw):
        if inc is None and meth == "dma_start" and self.dma_ev is not None:
            inc = self.dma_ev
        incs = []
        if inc is not None:
            inc.n += inc.step
            incs.append((inc.sem, inc.step))
        self.thunks.append(("o", meth, args, kw, incs))
    def replay(self, eng):
        for t in self.thunks:
            if t[0] == "w":
                eng.wait_ge(t[1], t[2])
            else:
                _, meth, args, kw, incs = t
                ins = getattr(eng, meth)(*args, **kw)
                for (sem, step) in incs:
                    ins.then_inc(sem, step)


class MK:
    def __init__(self, nc, stage):
        self.nc = nc
        self.stage = stage

    def declare(self, ctx):
        nc = self.nc
        dt = nc.dram_tensor
        ecm = ctx.enter_context
        self.x_h = dt("x", [TCORE, D], F32, kind="ExternalInput").ap()
        self.w_h = [dt(n, [D, D], F32, kind="ExternalInput").ap()
                    for n in ("wq", "wk", "wv", "wo")]
        self.rw_h = dt("rw", [E, D], F32, kind="ExternalInput").ap()
        self.rb_h = dt("rb", [E, 1], F32, kind="ExternalInput").ap()
        self.rng_h = dt("rng", [128, KC], F32, kind="ExternalInput").ap()
        self.rnb_h = dt("rnb", [128, KC], F32, kind="ExternalInput").ap()
        self.lng_h = dt("lng", [128, E, KC], F32, kind="ExternalInput").ap()
        self.lnb_h = dt("lnb", [128, E, KC], F32, kind="ExternalInput").ap()
        self.w1_h = dt("w1", [E, D, I], F32, kind="ExternalInput").ap()
        self.b1_h = dt("b1", [128, E, NIT], F32, kind="ExternalInput").ap()
        self.w2_h = dt("w2", [E, I, D], F32, kind="ExternalInput").ap()
        self.b2_h = dt("b2", [E, D], F32, kind="ExternalInput").ap()
        self.rc_h = dt("rc", [TCORE, 512], BF16, kind="ExternalInput").ap()
        self.rs_h = dt("rs", [TCORE, 512], BF16, kind="ExternalInput").ap()
        self.msk_h = dt("msk", [128, 128], F32, kind="ExternalInput").ap()
        self.idb_h = dt("idb_h", [128, 128], BF16, kind="ExternalInput").ap()
        self.pmk_h = dt("pmk", [128, E], F32, kind="ExternalInput").ap()
        self.out_h = dt("out", [TCORE, D], F32, kind="ExternalOutput").ap()
        if self.stage <= 2:
            self.dz_h = dt("d_z", [NTT, 128, KC, TT], BF16, kind="ExternalOutput").ap()
            self.dz2_h = dt("d_z2", [NTT, 128, KC, TT], BF16, kind="ExternalOutput").ap()
            self.dst_h = dt("d_st", [128, 8, 65], F32, kind="ExternalOutput").ap()
            self.dcc_h = dt("d_cc", [512, H * 65], F32, kind="ExternalOutput").ap()
        if self.stage == 2:
            self.dq_h = dt("d_qf", [128, 8, 512], BF16, kind="ExternalOutput").ap()
            self.dk_h = dt("d_kf", [128, 8, 512], BF16, kind="ExternalOutput").ap()
            self.dv_h = dt("d_vt", [128, 4, H * 65], BF16, kind="ExternalOutput").ap()
            self.dy_h = dt("d_yt", [128, 4, D], BF16, kind="ExternalOutput").ap()
            self.dc_h = dt("d_cmb", [128, KC, E], F32, kind="ExternalOutput").ap()

        self.s_wT = dt("s_wT", [4, 128, KC, D], BF16).ap()
        self.s_zfm = dt("s_zfm", [NTT, 128, KC, TT], BF16).ap()
        self.s_z2fm = dt("s_z2fm", [NTT, 128, KC, TT], BF16).ap()
        self.cc_in = dt("cc_in", [64, H * 65], F32).ap()
        self.cc_out = dt("cc_out", [8 * 64, H * 65], F32, addr_space="Shared").ap()

        sb = lambda n, s, d: ecm(nc.sbuf_tensor(n, s, d))
        self.zbuf = sb("zbuf", [128, KC, TT], BF16)
        self.hid = sb("hid", [128, NIT, 512], BF16)
        self.acc = sb("acc", [128, KC, D], F32)
        self.xs = sb("xs", [128, 2, D], F32)
        self.qf = sb("qf", [128, 8, 512], BF16)
        self.kf = sb("kf", [128, 8, 512], BF16)
        self.kt = sb("kt", [128, 4, D], BF16)
        self.vt = sb("vt", [128, 4, H * 65], BF16)
        self.yt = sb("yt", [128, 4, D], BF16)
        self.mstb = sb("mstb", [128, 2, 128], BF16)
        self.w1s = sb("w1s", [128, 4, KC, 128], BF16)
        self.w2s = sb("w2s", [128, 4, D], BF16)
        self.wts = sb("wts", [128, 2, KC, 512], BF16)
        self.rfc = sb("rfc", [128, 4, 512], BF16)
        self.rfs = sb("rfs", [128, 4, 512], BF16)
        self.scb = sb("scb", [128, 3, 512], BF16)
        self.sc3 = sb("sc3", [128, 2, 512], F32)
        self.z2e = sb("z2e", [128, KC, 512], BF16)
        self.st32 = sb("st32", [128, 8, 65], F32)
        self.stbf = sb("stbf", [128, 8, 65], BF16)

        self.cmbs = sb("cmbs", [128, KC, E], F32)
        self.cfm = sb("cfm", [8, KC, 128], BF16)
        self.mcol = sb("mcol", [128, 16], F32)
        self.emask = sb("emask", [128, 4, E], F32)
        self.idb = sb("idb", [128, 128], BF16)
        self.mskb = sb("mskb", [128, 128], F32)
        self.b1s = sb("b1s", [128, E, NIT], F32)
        self.lngs = sb("lngs", [128, E, KC], F32)
        self.lnbs = sb("lnbs", [128, E, KC], F32)
        self.b2s = sb("b2s", [E, D], BF16)
        self.rwT = sb("rwT", [128, KC, E], BF16)
        self.rngs = sb("rngs", [128, KC], F32)
        self.rnbs = sb("rnbs", [128, KC], F32)
        self.rnbb = sb("rnbb", [128, KC], BF16)
        self.cstc = sb("cstc", [E, 2], F32)
        self.cstbc = sb("cstbc", [E, 1], BF16)
        self.crow = sb("crow", [1, E], BF16)
        self.onesb = sb("onesb", [1, 128], BF16)
        self.pmks = sb("pmks", [128, E], F32)
        self.ps = ecm(nc.psum_tensor("ps", [128, 8, 512], F32))
        self.rwsb = self.z2e[0:8, 0:2, :].rearrange("p a b -> p (a b)")
        accf = self.acc[:].rearrange("p a b -> p (a b)")
        self.srb = accf[0:64, 0:2080].rearrange("p (a b) -> p a b", a=2)
        self.pstt = accf[0:64, 2080:2080 + H * 65]

        self.sem = {n: ecm(nc.semaphore(n)) for n in
                    ("sq", "gq", "pe", "dv", "ac", "cc")}
        self.T = Rec("t"); self.V = Rec("v"); self.S = Rec("s")
        self.G = Rec("g"); self.Y = Rec("y")
        self.e_sq = Ev(self.sem["sq"], 16); self.e_gq = Ev(self.sem["gq"], 16)
        self.Y.dma_ev = self.e_sq; self.G.dma_ev = self.e_gq
        self.e_pe = Ev(self.sem["pe"]); self.e_dv = Ev(self.sem["dv"])
        self.e_ac = Ev(self.sem["ac"]); self.e_cc = Ev(self.sem["cc"])

        self.xs_mark = [(0, 0), (0, 0)]
        self.yt_mark = [0, 0, 0, 0]
        self.tb_mark = [0, 0]
        self.pb_mark = [0, 0]
        self.stb_mark = [0, 0]
        self.ob_mark = [0, 0]
        self.sb_mark = [0, 0]
        self.mst_mark = [0, 0]
        self.wts_mark = [0, 0]
        self.w1s_mark = [0, 0, 0, 0]
        self.w2s_mark = [0, 0, 0, 0]
        self.scr_mark = [0]
        self.rf_mark = [0]
        self.zspill_mark = [0]
        self.zfm_ready = [0]
        self.z2_ready = [0]
        self.acc_store = [0]
        self.proj_bank = [0]
        self.proj_slot = [0]
        self.fb_mark = [0, 0]
        self.w2b_mark = [0] * 8
        self.rt_mark = [0]
        self.cf_mark = [0]
        self.b2_mark = [0]
        self.z2e_mark = [0]
        self.z_used = [0]
        self.wo_done = [0]
        self.hid_free = 0
        self.in_p2 = False

    def mmop(self, out, lhsT, rhs, start=True, stop=True, inc=None):
        self.T.op("matmul", out, lhsT, rhs, start=start, stop=stop, inc=inc)

    def trans(self, out_ps, in_sb, k=128, inc=None):
        self.T.op("matmul", out_ps, in_sb, self.idb[0:k, 0:k], start=True,
                  stop=True, inc=inc)

    def vsep(self):
        self.V.wait(self.e_dv, self.e_dv.mark())

    def ssep(self):
        self.S.wait(self.e_ac, self.e_ac.mark())

    def bank_fence(self):
        self.T.wait(self.e_dv, self.e_dv.mark())
        self.T.wait(self.e_ac, self.e_ac.mark())

    def phase0(self):
        T, V, S, G, Y = self.T, self.V, self.S, self.G, self.Y
        ps, idb = self.ps, self.idb
        for (dstt, src) in ((idb[:], self.idb_h), (self.mskb[:], self.msk_h),
                            (self.b1s[:], self.b1_h), (self.lngs[:], self.lng_h),
                            (self.lnbs[:], self.lnb_h), (self.rngs[:], self.rng_h),
                            (self.rnbs[:], self.rnb_h), (self.pmks[:], self.pmk_h),
                            (self.cstc[:, 0:1], self.rb_h)):
            Y.op("dma_start", out=dstt, in_=src, inc=self.e_sq)
        G.op("dma_start", out=self.b2s[:], in_=self.b2_h, inc=self.e_gq)
        G.op("dma_start", out=self.rwsb, in_=self.rw_h, inc=self.e_gq)
        m_ld0 = self.e_sq.mark()

        V.wait(self.e_sq, m_ld0)
        V.op("tensor_copy", self.rnbb[:], self.rnbs[:])
        V.op("memset", self.onesb[:], 1.0)
        ones65 = self.vt[:].rearrange("p a (h q) -> p a h q", q=65)[:, :, :, 64:65]
        V.op("memset", ones65, 1.0)
        V.op("memset", self.mcol[:, 15:16], EPS)
        V.op("memset", self.st32[:], 0.0, inc=self.e_dv)
        m_init = self.e_dv.mark()

        hidf = self.hid[:].rearrange("p a b -> p (a b)")
        wst = hidf[:, 0:8192].rearrange("p (kc j) -> p kc j", kc=KC)
        m_wst_free = 0
        for m in range(4):
            G.wait(self.e_pe, self.e_pe.mark())
            G.op("dma_start", out=self.zbuf[:],
                 in_=self.w_h[m].rearrange("(jc p) d -> p jc d", p=128), inc=self.e_gq)
            T.wait(self.e_gq, self.e_gq.mark())
            T.wait(self.e_sq, m_ld0)
            T.wait(self.e_sq, m_wst_free)
            cnt = 0
            for jc in range(8):
                for kc in range(KC):
                    b = cnt % 2; cnt += 1
                    T.wait(self.e_dv, self.tb_mark[b])
                    self.trans(ps[:, b, 0:128],
                               self.zbuf[:, jc, 128 * kc:128 * (kc + 1)], inc=self.e_pe)
                    V.wait(self.e_pe, self.e_pe.mark())
                    V.op("tensor_copy", wst[:, kc, 128 * jc:128 * (jc + 1)],
                         ps[:, b, 0:128], inc=self.e_dv)
                    self.tb_mark[b] = self.e_dv.mark()
            Y.wait(self.e_dv, self.e_dv.mark())
            Y.op("dma_start", out=self.s_wT[m], in_=wst, inc=self.e_sq)
            m_wst_free = self.e_sq.mark()
        self.hid_free = m_wst_free

        T.wait(self.e_gq, self.e_gq.mark())
        T.wait(self.e_dv, m_init)
        for kc in range(KC):
            b = 4 + (kc % 2)
            if kc >= 2:
                T.wait(self.e_dv, self.e_dv.mark() - 2)
            self.trans(ps[:, b, 0:8], self.rwsb[:, 128 * kc:128 * (kc + 1)], k=8,
                       inc=self.e_pe)
            V.wait(self.e_pe, self.e_pe.mark())
            V.op("tensor_scalar", self.rwT[:, kc, :], ps[:, b, 0:8],
                 self.rngs[:, kc:kc + 1], None, ALU.mult)
            V.op("tensor_copy", self.scb[:, 0, 8 * kc:8 * (kc + 1)], ps[:, b, 0:8],
                 inc=self.e_dv)
        T.wait(self.e_dv, self.e_dv.mark())
        for kc in range(KC):
            self.mmop(ps[0:8, 6, 0:1], self.scb[:, 0, 8 * kc:8 * (kc + 1)],
                      self.rnbb[:, kc:kc + 1], start=(kc == 0), stop=(kc == KC - 1),
                      inc=(self.e_pe if kc == KC - 1 else None))
        V.wait(self.e_pe, self.e_pe.mark())
        V.wait(self.e_sq, m_ld0)
        V.op("tensor_tensor", self.cstc[:, 0:1], self.cstc[:, 0:1], ps[0:8, 6, 0:1],
             ALU.add, inc=self.e_dv)
        self.vsep()
        V.op("tensor_copy", self.cstbc[:], self.cstc[:, 0:1], inc=self.e_dv)
        T.wait(self.e_dv, self.e_dv.mark())
        self.trans(ps[0:1, 7, 0:8], self.cstbc[:], k=8, inc=self.e_pe)
        V.wait(self.e_pe, self.e_pe.mark())
        V.op("tensor_copy", self.crow[:], ps[0:1, 7, 0:8], inc=self.e_dv)

    def ln_stats(self, src):
        V, S = self.V, self.S
        mc = self.mcol
        sc3f = self.sc3[:].rearrange("p a b -> p (a b)")
        V.op("reduce_sum", mc[:, 0:1], src, AX.X, inc=self.e_dv)
        S.wait(self.e_dv, self.e_dv.mark())
        S.op("square", sc3f, src, inc=self.e_ac)
        V.wait(self.e_ac, self.e_ac.mark())
        V.op("reduce_sum", mc[:, 1:2], sc3f, AX.X, inc=self.e_dv)
        S.wait(self.e_dv, self.e_dv.mark())
        S.op("activation", mc[:, 2:3], mc[:, 0:1], AF.Identity, scale=1.0 / D)
        S.op("activation", mc[:, 3:4], mc[:, 1:2], AF.Identity, scale=1.0 / D,
             inc=self.e_ac)
        V.wait(self.e_ac, self.e_ac.mark())
        V.op("scalar_tensor_tensor", mc[:, 4:5], mc[:, 2:3], mc[:, 2:3],
             mc[:, 3:4], ALU.mult, ALU.subtract, inc=self.e_dv)
        S.wait(self.e_dv, self.e_dv.mark())
        S.op("activation", mc[:, 6:7], mc[:, 4:5], AF.Sqrt, bias=mc[:, 15:16],
             scale=-1.0, inc=self.e_ac)
        V.wait(self.e_ac, self.e_ac.mark())
        V.op("reciprocal", mc[:, 7:8], mc[:, 6:7], inc=self.e_dv)
        self.vsep()
        V.op("scalar_tensor_tensor", mc[:, 8:9], mc[:, 2:3], -1.0, mc[:, 7:8],
             ALU.mult, ALU.mult, inc=self.e_dv)

    def tok_to_fm(self, src_tok, dst_fm, sub):
        T, V = self.T, self.V
        ps = self.ps
        for c2 in range(4):
            b = 2 + (c2 % 2)
            T.wait(self.e_dv, self.tb_mark[c2 % 2])
            self.trans(ps[:, b, 0:128], src_tok[:, 256 * c2:256 * c2 + 128])
            self.trans(ps[:, b, 128:256], src_tok[:, 256 * c2 + 128:256 * (c2 + 1)],
                       inc=self.e_pe)
            V.wait(self.e_pe, self.e_pe.mark())
            V.op("tensor_copy",
                 dst_fm[:, 2 * c2:2 * c2 + 2, 128 * sub:128 * (sub + 1)],
                 ps[:, b, 0:256].rearrange("p (a q) -> p a q", a=2), inc=self.e_dv)
            self.tb_mark[c2 % 2] = self.e_dv.mark()

    def ln_subtile(self, tt, sub):
        T, V, S, Y = self.T, self.V, self.S, self.Y
        slot = sub % 2
        ztk = self.yt[:, slot, :]
        z2tk = self.yt[:, 2 + slot, :]
        Y.wait(self.e_dv, self.xs_mark[slot][0])
        Y.wait(self.e_ac, self.xs_mark[slot][1])
        Y.op("dma_start", out=self.xs[:, slot],
             in_=self.x_h[(tt * 8 + sub) * 128:(tt * 8 + sub + 1) * 128, :],
             inc=self.e_sq)
        m_x = self.e_sq.mark()
        V.wait(self.e_sq, m_x)
        S.wait(self.e_sq, m_x)
        self.ln_stats(self.xs[:, slot])
        S.wait(self.e_dv, self.e_dv.mark())
        S.wait(self.e_pe, self.yt_mark[slot])
        S.op("activation", ztk, self.xs[:, slot], AF.Identity,
             bias=self.mcol[:, 8:9], scale=self.mcol[:, 7:8], inc=self.e_ac)
        m_z = self.e_ac.mark()
        self.xs_mark[slot] = (self.e_dv.mark(), self.e_ac.mark())
        V.wait(self.e_ac, m_z)
        self.ln_stats(ztk)
        S.wait(self.e_dv, self.e_dv.mark())
        S.wait(self.e_pe, self.yt_mark[2 + slot])
        S.op("activation", z2tk, ztk, AF.Identity,
             bias=self.mcol[:, 8:9], scale=self.mcol[:, 7:8], inc=self.e_ac)
        m_z2 = self.e_ac.mark()
        hidf = self.hid[:].rearrange("p a b -> p (a b)")
        z2st = hidf[:, 0:8192].rearrange("p (c t) -> p c t", c=KC)
        T.wait(self.e_ac, m_z)
        T.wait(self.e_sq, self.zspill_mark[0])
        if tt == 0 and sub == 0:
            T.wait(self.e_sq, self.hid_free)
        self.tok_to_fm(ztk, self.zbuf, sub)
        self.yt_mark[slot] = self.e_pe.mark()
        T.wait(self.e_ac, m_z2)
        self.tok_to_fm(z2tk, z2st, sub)
        self.yt_mark[2 + slot] = self.e_pe.mark()

    def load_fields(self, tt, hh):
        Y = self.Y
        Y.wait(self.e_dv, self.rf_mark[0])
        base = tt * TT + 512 * hh
        Y.op("dma_start", out=self.rfc[:],
             in_=self.rc_h[base:base + 512, :].rearrange("(a p) f -> p a f", p=128))
        Y.op("dma_start", out=self.rfs[:],
             in_=self.rs_h[base:base + 512, :].rearrange("(a p) f -> p a f", p=128),
             inc=self.e_sq)
        return self.e_sq.mark()

    def rope_phi(self, tc4, seg, psb, dest, m_field, trans_to):
        T, V, S = self.T, self.V, self.S
        ps = self.ps
        s0, s1, s2 = self.scb[:, 0], self.scb[:, 1], self.scb[:, 2]
        rs = lambda t: t.rearrange("p (h i) -> p h i", i=64)
        A0, B0 = rs(s0)[:, :, 0:32], rs(s0)[:, :, 32:64]
        A1, B1 = rs(s1)[:, :, 0:32], rs(s1)[:, :, 32:64]
        A2, B2 = rs(s2)[:, :, 0:32], rs(s2)[:, :, 32:64]
        FA = self.rfc[:, tc4].rearrange("p (h i) -> p h i", i=32)[:, 0:8, :]
        FS = self.rfs[:, tc4].rearrange("p (h i) -> p h i", i=32)[:, 0:8, :]
        V.wait(self.e_sq, m_field)
        V.wait(self.e_pe, self.scr_mark[0])
        V.op("tensor_copy", s0, ps[:, psb, :], inc=self.e_dv)
        m_psum_free = self.e_dv.mark()
        self.vsep()
        V.op("tensor_tensor", A1, A0, FA, ALU.mult)
        V.op("tensor_tensor", A2, A0, FS, ALU.mult)
        V.op("tensor_tensor", B2, B0, FA, ALU.mult)
        V.op("tensor_tensor", B1, B0, FS, ALU.mult)
        V.op("tensor_tensor", A1, A1, B1, ALU.subtract)
        V.thunks.pop()
        V.op("tensor_tensor", B2, A2, B2, ALU.add, inc=self.e_dv)
        self.vsep()
        V.op("tensor_tensor", A1, A1, B1, ALU.subtract)
        V.op("tensor_copy", B1, B2, inc=self.e_dv)
        self.vsep()
        V.op("tensor_scalar", self.sc3[:, 0], s1, 0.0, None, ALU.min, inc=self.e_dv)
        S.wait(self.e_dv, self.e_dv.mark())
        S.op("activation", self.sc3[:, 1], self.sc3[:, 0], AF.Exp, inc=self.e_ac)
        V.wait(self.e_ac, self.e_ac.mark())
        V.op("scalar_tensor_tensor", dest, s1, 0.0, self.sc3[:, 1],
             ALU.max, ALU.add, inc=self.e_dv)
        if trans_to is not None:
            m_dest = self.e_dv.mark()
            T.wait(self.e_dv, m_dest)
            b = 2 + (seg % 2)
            T.wait(self.e_dv, self.tb_mark[seg % 2])
            for jb in range(4):
                self.trans(ps[:, b, 128 * jb:128 * (jb + 1)],
                           dest[:, 128 * jb:128 * (jb + 1)],
                           inc=(self.e_pe if jb == 3 else None))
            self.scr_mark[0] = self.e_pe.mark()
            V.wait(self.e_pe, self.e_pe.mark())
            V.op("tensor_copy",
                 trans_to[:, 4 * seg:4 * seg + 4, 128 * tc4:128 * (tc4 + 1)],
                 ps[:, b, :].rearrange("p (a q) -> p a q", a=4), inc=self.e_dv)
            self.tb_mark[seg % 2] = self.e_dv.mark()
        return m_psum_free

    def proj_half(self, hh, which, m_idx, m_field):
        T, V, Y = self.T, self.V, self.Y
        ps = self.ps
        V.wait(self.e_pe, self.e_pe.mark())
        for seg in range(2):
            slot = self.proj_slot[0] % 2; self.proj_slot[0] += 1
            Y.wait(self.e_pe, self.wts_mark[slot])
            Y.op("dma_start", out=self.wts[:, slot],
                 in_=self.s_wT[m_idx][:, :, 512 * seg:512 * (seg + 1)], inc=self.e_sq)
            m_w = self.e_sq.mark()
            for tc in range(4):
                tcg = 4 * hh + tc
                b = self.proj_bank[0] % 2; self.proj_bank[0] += 1
                T.wait(self.e_sq, m_w)
                T.wait(self.e_dv, self.pb_mark[b])
                T.wait(self.e_dv, self.zfm_ready[0])
                for kc in range(KC):
                    self.mmop(ps[:, b, :], self.zbuf[:, kc, 128 * tcg:128 * (tcg + 1)],
                              self.wts[:, slot, kc, :], start=(kc == 0),
                              stop=(kc == KC - 1),
                              inc=(self.e_pe if kc == KC - 1 else None))
                if tc == 3:
                    self.wts_mark[slot] = self.e_pe.mark()
                V.wait(self.e_pe, self.e_pe.mark())
                if which == "v":
                    vvt = self.vt[:, tc].rearrange("p (h q) -> p h q", q=65)
                    V.op("tensor_copy", vvt[:, 8 * seg:8 * seg + 8, 0:64],
                         ps[:, b, :].rearrange("p (h e) -> p h e", e=64),
                         inc=self.e_dv)
                    self.pb_mark[b] = self.e_dv.mark()
                else:
                    if which == "k":
                        dest = self.kt[:, tc, 512 * seg:512 * (seg + 1)]
                        tr = self.kf if self.in_p2 else None
                    else:
                        dest = self.scb[:, 1]
                        tr = self.qf
                    self.pb_mark[b] = self.rope_phi(tc, seg, b, dest, m_field, tr)

    def state_chunk(self, tc, with_stbf):
        T, V = self.T, self.V
        ps = self.ps
        T.wait(self.e_dv, self.e_dv.mark())
        for pr in range(8):
            b = 6 + (pr % 2)
            T.wait(self.e_dv, self.stb_mark[pr % 2])
            self.mmop(ps[0:64, b, 0:65], self.kt[:, tc, 128 * pr:128 * pr + 64],
                      self.vt[:, tc, 65 * 2 * pr:65 * 2 * pr + 65])
            self.mmop(ps[64:128, b, 0:65], self.kt[:, tc, 128 * pr + 64:128 * (pr + 1)],
                      self.vt[:, tc, 65 * (2 * pr + 1):65 * (2 * pr + 1) + 65],
                      inc=self.e_pe)
            V.wait(self.e_pe, self.e_pe.mark())
            V.op("tensor_tensor", self.st32[:, pr], self.st32[:, pr], ps[:, b, 0:65],
                 ALU.add, inc=self.e_dv)
            if with_stbf:
                self.vsep()
                V.op("tensor_copy", self.stbf[:, pr], self.st32[:, pr], inc=self.e_dv)
            self.stb_mark[pr % 2] = self.e_dv.mark()

    def phase1(self):
        self.in_p2 = False
        for tt in range(NTT):
            for sub in range(8):
                self.ln_subtile(tt, sub)
            self.zfm_ready[0] = self.e_dv.mark()
            for hh in range(2):
                m_field = self.load_fields(tt, hh)
                self.proj_half(hh, "k", 1, m_field)
                self.proj_half(hh, "v", 2, m_field)
                self.bank_fence()
                for tc in range(4):
                    self.state_chunk(tc, with_stbf=False)
                self.rf_mark[0] = self.e_dv.mark()
            Y = self.Y
            Y.wait(self.e_dv, self.e_dv.mark())
            Y.op("dma_start", out=self.s_zfm[tt], in_=self.zbuf[:])
            hidf = self.hid[:].rearrange("p a b -> p (a b)")
            z2st = hidf[:, 0:8192].rearrange("p (c t) -> p c t", c=KC)
            Y.op("dma_start", out=self.s_z2fm[tt], in_=z2st, inc=self.e_sq)
            self.zspill_mark[0] = self.e_sq.mark()

    def phase_cc(self):
        V, G, Y = self.V, self.G, self.Y
        cci = self.cc_in.rearrange("p (pr q) -> p pr q", q=130)
        Y.wait(self.e_dv, self.e_dv.mark())
        Y.op("dma_start", out=cci[:, :, 0:65], in_=self.st32[0:64, :, :])
        Y.op("dma_start", out=cci[:, :, 65:130], in_=self.st32[64:128, :, :],
             inc=self.e_sq)
        G.wait(self.e_sq, self.e_sq.mark())
        G.op("collective_compute", "AllGather", ALU.bypass,
             replica_groups=[list(range(8))], ins=[self.cc_in[:]],
             outs=[self.cc_out[:]], inc=self.e_cc)
        srm = [0, 0]
        for r in range(8):
            Y.wait(self.e_cc, 1)
            Y.wait(self.e_dv, srm[r % 2])
            Y.op("dma_start", out=self.srb[:, r % 2],
                 in_=self.cc_out[64 * r:64 * (r + 1), :], inc=self.e_sq)
            m_r = self.e_sq.mark()
            V.wait(self.e_sq, m_r)
            if r == 0:
                V.op("tensor_scalar", self.pstt, self.srb[:, 0],
                     self.pmks[0:64, 0:1], None, ALU.mult, inc=self.e_dv)
            else:
                self.vsep()
                V.op("scalar_tensor_tensor", self.pstt, self.srb[:, r % 2],
                     self.pmks[0:64, r:r + 1], self.pstt, ALU.mult, ALU.add,
                     inc=self.e_dv)
            srm[r % 2] = self.e_dv.mark()
        pstv = self.pstt.rearrange("p (pr q) -> p pr q", q=130)
        Y.wait(self.e_dv, self.e_dv.mark())
        Y.op("dma_start", out=self.st32[0:64, :, :], in_=pstv[:, :, 0:65])
        Y.op("dma_start", out=self.st32[64:128, :, :], in_=pstv[:, :, 65:130],
             inc=self.e_sq)
        V.wait(self.e_sq, self.e_sq.mark())
        V.op("tensor_copy", self.stbf[:], self.st32[:], inc=self.e_dv)

    def scan_chunk(self, tc):
        T, V = self.T, self.V
        ps = self.ps
        for grp in range(4):
            ob = 3 if grp % 2 == 0 else 5
            for j in range(4):
                h = 4 * grp + j
                pr = h // 2; od = h % 2
                sb = 2 if od == 0 else 4
                T.wait(self.e_dv, self.sb_mark[od])
                self.mmop(ps[:, sb, 0:128],
                          self.kf[64 * od:64 * (od + 1), pr, 128 * tc:128 * (tc + 1)],
                          self.qf[64 * od:64 * (od + 1), pr, 128 * tc:128 * (tc + 1)],
                          inc=self.e_pe)
                V.wait(self.e_pe, self.e_pe.mark())
                V.wait(self.e_pe, self.mst_mark[od])
                V.op("tensor_tensor", self.mstb[:, od], ps[:, sb, 0:128], self.mskb[:],
                     ALU.mult, inc=self.e_dv)
                self.sb_mark[od] = self.e_dv.mark()
                T.wait(self.e_dv, self.sb_mark[od])
                if j == 0:
                    T.wait(self.e_dv, self.ob_mark[grp % 2])
                self.mmop(ps[:, ob, 65 * j:65 * j + 65], self.mstb[:, od],
                          self.vt[:, tc, 65 * h:65 * h + 65], start=True, stop=False)
                self.mmop(ps[:, ob, 65 * j:65 * j + 65],
                          self.qf[64 * od:64 * (od + 1), pr, 128 * tc:128 * (tc + 1)],
                          self.stbf[64 * od:64 * (od + 1), pr, :],
                          start=False, stop=True, inc=self.e_pe)
                self.mst_mark[od] = self.e_pe.mark()
            m_o = self.e_pe.mark()
            V.wait(self.e_pe, m_o)
            dcols = ps[:, ob, 0:260].rearrange("p (j q) -> p j q", q=65)[:, :, 64:65] \
                .rearrange("p a b -> p (a b)")
            V.op("tensor_scalar", self.mcol[:, 9:13], dcols, EPSA, None, ALU.max,
                 inc=self.e_dv)
            self.vsep()
            V.op("reciprocal", self.mcol[:, 9:13], self.mcol[:, 9:13], inc=self.e_dv)
            self.vsep()
            for j in range(4):
                h = 4 * grp + j
                V.op("tensor_scalar", self.yt[:, tc, 64 * h:64 * (h + 1)],
                     ps[:, ob, 65 * j:65 * j + 64], self.mcol[:, 9 + j:10 + j],
                     None, ALU.mult, inc=self.e_dv)
            self.ob_mark[grp % 2] = self.e_dv.mark()
        self.state_chunk(tc, with_stbf=True)

    def wo_and_accinit(self, tt, hh):
        T, V, Y = self.T, self.V, self.Y
        ps = self.ps
        for tc in range(4):
            T.wait(self.e_dv, self.e_dv.mark())
            for jb2 in range(4):
                b = 2 + (jb2 % 2)
                T.wait(self.e_dv, self.tb_mark[jb2 % 2])
                self.trans(ps[:, b, 0:128],
                           self.yt[:, tc, 256 * jb2:256 * jb2 + 128])
                self.trans(ps[:, b, 128:256],
                           self.yt[:, tc, 256 * jb2 + 128:256 * (jb2 + 1)],
                           inc=self.e_pe)
                V.wait(self.e_pe, self.e_pe.mark())
                V.op("tensor_copy",
                     self.qf[:, 2 * jb2:2 * jb2 + 2, 128 * tc:128 * (tc + 1)],
                     ps[:, b, 0:256].rearrange("p (a q) -> p a q", a=2),
                     inc=self.e_dv)
                self.tb_mark[jb2 % 2] = self.e_dv.mark()
        m_yf = self.e_dv.mark()
        m_w = [0, 0]
        for seg in range(2):
            slot = self.proj_slot[0] % 2; self.proj_slot[0] += 1
            Y.wait(self.e_pe, self.wts_mark[slot])
            Y.op("dma_start", out=self.wts[:, slot],
                 in_=self.s_wT[3][:, :, 512 * seg:512 * (seg + 1)], inc=self.e_sq)
            m_w[slot] = self.e_sq.mark()
        for tc in range(4):
            tcg = 4 * hh + tc
            slot_x = tcg % 2
            Y.wait(self.e_dv, self.xs_mark[slot_x][0])
            Y.wait(self.e_ac, self.xs_mark[slot_x][1])
            Y.op("dma_start", out=self.xs[:, slot_x],
                 in_=self.x_h[(tt * 8 + tcg) * 128:(tt * 8 + tcg + 1) * 128, :],
                 inc=self.e_sq)
            m_x = self.e_sq.mark()
            for seg in range(2):
                slot = (self.proj_slot[0] - 2 + seg) % 2
                b = seg
                T.wait(self.e_sq, m_w[slot])
                T.wait(self.e_dv, m_yf)
                T.wait(self.e_dv, self.pb_mark[b])
                for pr in range(8):
                    self.mmop(ps[:, b, :], self.qf[:, pr, 128 * tc:128 * (tc + 1)],
                              self.wts[:, slot, pr, :], start=(pr == 0),
                              stop=(pr == 7), inc=(self.e_pe if pr == 7 else None))
                V.wait(self.e_pe, self.e_pe.mark())
                V.wait(self.e_sq, m_x)
                V.wait(self.e_sq, self.acc_store[0])
                V.op("tensor_tensor", self.acc[:, tcg, 512 * seg:512 * (seg + 1)],
                     ps[:, b, :], self.xs[:, slot_x, 512 * seg:512 * (seg + 1)],
                     ALU.add, inc=self.e_dv)
                self.pb_mark[b] = self.e_dv.mark()
            self.xs_mark[slot_x] = (self.e_dv.mark(), self.e_ac.mark())
        self.wts_mark[0] = self.e_pe.mark()
        self.wts_mark[1] = self.e_pe.mark()
        self.wo_done[0] = self.e_pe.mark()

    def router_tile(self, tcg):
        T, V, S = self.T, self.V, self.S
        ps = self.ps
        em = self.emask
        ltok, eq1, eq2, msked = em[:, 0], em[:, 1], em[:, 2], em[:, 3]
        mc = self.mcol
        T.wait(self.e_sq, self.z2_ready[0])
        T.wait(self.e_dv, self.rt_mark[0])
        for kc in range(KC):
            self.mmop(ps[:, 4, 0:8], self.zbuf[:, kc, 128 * tcg:128 * (tcg + 1)],
                      self.rwT[:, kc, :], start=(kc == 0), stop=False)
        self.mmop(ps[:, 4, 0:8], self.onesb[:], self.crow[:], start=False,
                  stop=True, inc=self.e_pe)
        V.wait(self.e_pe, self.e_pe.mark())
        V.op("tensor_copy", ltok, ps[:, 4, 0:8])
        V.op("reduce_max", mc[:, 9:10], ps[:, 4, 0:8], AX.X, inc=self.e_dv)
        self.rt_mark[0] = self.e_dv.mark()
        self.vsep()
        V.op("tensor_scalar", eq1, ltok, mc[:, 9:10], None, ALU.is_equal,
             inc=self.e_dv)
        self.vsep()
        V.op("scalar_tensor_tensor", msked, eq1, -1.0e30, ltok, ALU.mult, ALU.add,
             inc=self.e_dv)
        self.vsep()
        V.op("reduce_max", mc[:, 10:11], msked, AX.X, inc=self.e_dv)
        self.vsep()
        V.op("tensor_scalar", eq2, ltok, mc[:, 10:11], None, ALU.is_equal)
        V.op("tensor_tensor", mc[:, 11:12], mc[:, 10:11], mc[:, 9:10], ALU.subtract,
             inc=self.e_dv)
        S.wait(self.e_dv, self.e_dv.mark())
        S.op("activation", mc[:, 13:14], mc[:, 11:12], AF.Sigmoid, scale=-1.0)
        S.op("activation", mc[:, 14:15], mc[:, 11:12], AF.Sigmoid, inc=self.e_ac)
        V.wait(self.e_ac, self.e_ac.mark())
        V.op("tensor_scalar", eq1, eq1, mc[:, 13:14], None, ALU.mult, inc=self.e_dv)
        self.vsep()
        V.op("scalar_tensor_tensor", self.cmbs[:, tcg], eq2, mc[:, 14:15], eq1,
             ALU.mult, ALU.add, inc=self.e_dv)
        self.vsep()
        V.op("tensor_copy", self.scb[:, 0, 0:8], self.cmbs[:, tcg], inc=self.e_dv)
        m_cb = self.e_dv.mark()
        T.wait(self.e_dv, m_cb)
        T.wait(self.e_dv, self.cf_mark[0])
        self.trans(ps[0:8, 5, 0:128], self.scb[:, 0, 0:8], inc=self.e_pe)
        V.wait(self.e_pe, self.e_pe.mark())
        V.op("tensor_copy", self.cfm[:, tcg, :], ps[0:8, 5, 0:128], inc=self.e_dv)
        self.cf_mark[0] = self.e_dv.mark()

    def b2_term(self):
        T, V = self.T, self.V
        ps = self.ps
        for tcg in range(8):
            for seg in range(2):
                T.wait(self.e_dv, self.b2_mark[0])
                self.mmop(ps[:, 6, :], self.cfm[:, tcg, :],
                          self.b2s[:, 512 * seg:512 * (seg + 1)], inc=self.e_pe)
                V.wait(self.e_pe, self.e_pe.mark())
                V.wait(self.e_dv, 0)
                V.op("tensor_tensor", self.acc[:, tcg, 512 * seg:512 * (seg + 1)],
                     self.acc[:, tcg, 512 * seg:512 * (seg + 1)], ps[:, 6, :],
                     ALU.add, inc=self.e_dv)
                self.b2_mark[0] = self.e_dv.mark()

    def ffn_expert(self, tt, e):
        T, V, S, G = self.T, self.V, self.S, self.G
        ps = self.ps
        for hh in range(2):
            V.wait(self.e_sq, self.z2_ready[0])
            V.wait(self.e_pe, self.z2e_mark[0])
            for kc in range(KC):
                V.op("tensor_scalar", self.z2e[:, kc],
                     self.zbuf[:, kc, 512 * hh:512 * (hh + 1)],
                     self.lngs[:, e, kc:kc + 1], self.lnbs[:, e, kc:kc + 1],
                     ALU.mult, op1=ALU.add,
                     inc=(self.e_dv if kc == KC - 1 else None))
            m_z2e = self.e_dv.mark()
            for it in range(NIT):
                slot = it % 4
                G.wait(self.e_pe, self.w1s_mark[slot])
                G.op("dma_start", out=self.w1s[:, slot],
                     in_=self.w1_h[e][:, 128 * it:128 * (it + 1)]
                     .rearrange("(c p) i -> p c i", p=128), inc=self.e_gq)
                m_wl = self.e_gq.mark()
                fb = 6 + (it % 2)
                T.wait(self.e_gq, m_wl)
                T.wait(self.e_dv, m_z2e)
                T.wait(self.e_ac, self.fb_mark[it % 2])
                if it < 2:
                    T.wait(self.e_dv, self.w2b_mark[6 + it])
                for kc in range(KC):
                    self.mmop(ps[:, fb, :], self.w1s[:, slot, kc], self.z2e[:, kc],
                              start=(kc == 0), stop=(kc == KC - 1),
                              inc=(self.e_pe if kc == KC - 1 else None))
                self.w1s_mark[slot] = self.e_pe.mark()
                S.wait(self.e_pe, self.e_pe.mark())
                if tt == 0 and e == 0 and hh == 0 and it == 0:
                    S.wait(self.e_sq, self.zspill_mark[0])
                S.op("activation", self.hid[:, it], ps[:, fb, :], AF.Gelu,
                     bias=self.b1s[:, e, it:it + 1], inc=self.e_ac)
                self.fb_mark[it % 2] = self.e_ac.mark()
            self.z2e_mark[0] = self.e_pe.mark()
            m_gelu_all = self.e_ac.mark()
            for ic in range(NIT):
                slot = ic % 4
                G.wait(self.e_pe, self.w2s_mark[slot])
                G.op("dma_start", out=self.w2s[:, slot],
                     in_=self.w2_h[e][128 * ic:128 * (ic + 1), :], inc=self.e_gq)
                m_wl = self.e_gq.mark()
                T.wait(self.e_gq, m_wl)
                if ic == 0:
                    T.wait(self.e_ac, m_gelu_all)
                    for bb in range(8):
                        T.wait(self.e_dv, self.w2b_mark[bb])
                for tc in range(4):
                    for seg in range(2):
                        bb = tc * 2 + seg
                        self.mmop(ps[:, bb, :],
                                  self.hid[:, ic, 128 * tc:128 * (tc + 1)],
                                  self.w2s[:, slot, 512 * seg:512 * (seg + 1)],
                                  start=(ic == 0), stop=(ic == NIT - 1),
                                  inc=(self.e_pe if (tc, seg) == (3, 1) else None))
                self.w2s_mark[slot] = self.e_pe.mark()
            m_w2 = self.e_pe.mark()
            V.wait(self.e_pe, m_w2)
            for bb in (6, 7, 0, 1, 2, 3, 4, 5):
                tc, seg = bb // 2, bb % 2
                tcg = 4 * hh + tc
                V.op("scalar_tensor_tensor",
                     self.acc[:, tcg, 512 * seg:512 * (seg + 1)],
                     ps[:, bb, :], self.cmbs[:, tcg, e:e + 1],
                     self.acc[:, tcg, 512 * seg:512 * (seg + 1)],
                     ALU.mult, ALU.add, inc=self.e_dv)
                self.w2b_mark[bb] = self.e_dv.mark()

    def phase2(self):
        self.in_p2 = True
        T, V, S, G, Y = self.T, self.V, self.S, self.G, self.Y
        for tt in range(NTT):
            Y.wait(self.e_pe, self.z_used[0])
            Y.wait(self.e_dv, self.zq_used_dv)
            Y.op("dma_start", out=self.zbuf[:], in_=self.s_zfm[tt], inc=self.e_sq)
            m_z = self.e_sq.mark()
            self.zfm_ready[0] = 0
            T.wait(self.e_sq, m_z)
            for hh in range(2):
                m_field = self.load_fields(tt, hh)
                self.bank_fence()
                self.proj_half(hh, "k", 1, m_field)
                self.proj_half(hh, "v", 2, m_field)
                self.scr_mark[0] = max(self.scr_mark[0], self.wo_done[0])
                self.proj_half(hh, "q", 0, m_field)
                if hh == 1:
                    self.z_used[0] = self.e_pe.mark()
                    Y.wait(self.e_pe, self.z_used[0])
                    Y.op("dma_start", out=self.zbuf[:], in_=self.s_z2fm[tt],
                         inc=self.e_sq)
                    self.z2_ready[0] = self.e_sq.mark()
                self.bank_fence()
                for tc in range(4):
                    self.scan_chunk(tc)
                self.bank_fence()
                self.wo_and_accinit(tt, hh)
                self.rf_mark[0] = self.e_dv.mark()
            self.bank_fence()
            if self.stage == 2 and tt == 0:
                self.dump_attn()
            for tcg in range(8):
                self.router_tile(tcg)
            self.b2_term()
            self.bank_fence()
            if self.stage >= 3:
                for e in range(E):
                    self.ffn_expert(tt, e)
            Y.wait(self.e_dv, self.e_dv.mark())
            Y.op("dma_start",
                 out=self.out_h[tt * TT:(tt + 1) * TT, :]
                 .rearrange("(tc p) d -> p tc d", p=128),
                 in_=self.acc[:], inc=self.e_sq)
            self.acc_store[0] = self.e_sq.mark()
            self.z_used[0] = self.e_pe.mark()
            self.zq_used_dv = self.e_dv.mark()
        Y.wait(self.e_sq, self.e_sq.mark())

    def dump_attn(self):
        Y = self.Y
        Y.wait(self.e_dv, self.e_dv.mark())
        for (dst, src) in ((self.dq_h, self.qf[:]), (self.dk_h, self.kf[:]),
                           (self.dv_h, self.vt[:]), (self.dy_h, self.yt[:])):
            Y.op("dma_start", out=dst, in_=src, inc=self.e_sq)

    def emit(self):
        self.zq_used_dv = 0
        self.phase0()
        self.phase1()
        self.phase_cc()
        if self.stage <= 2:
            Y = self.Y
            Y.op("dma_start", out=self.dcc_h, in_=self.cc_out, inc=self.e_sq)
            Y.op("dma_start", out=self.dz_h, in_=self.s_zfm, inc=self.e_sq)
            Y.op("dma_start", out=self.dz2_h, in_=self.s_z2fm, inc=self.e_sq)
            Y.wait(self.e_dv, self.e_dv.mark())
            Y.op("dma_start", out=self.dst_h, in_=self.st32[:], inc=self.e_sq)
        if self.stage == 1:
            V, Y = self.V, self.Y
            V.op("memset", self.acc[:], 0.0, inc=self.e_dv)
            Y.wait(self.e_dv, self.e_dv.mark())
            for tt in range(NTT):
                Y.op("dma_start",
                     out=self.out_h[tt * TT:(tt + 1) * TT, :]
                     .rearrange("(tc p) d -> p tc d", p=128),
                     in_=self.acc[:], inc=self.e_sq)
            Y.wait(self.e_sq, self.e_sq.mark())
        else:
            self.phase2()
        if self.stage == 2:
            Y = self.Y
            Y.wait(self.e_dv, self.e_dv.mark())
            Y.op("dma_start", out=self.dc_h, in_=self.cmbs[:], inc=self.e_sq)
            Y.wait(self.e_sq, self.e_sq.mark())


def build_nc(stage=3):
    nc = bass.Bass("TRN2", target_bir_lowering=False, debug=False)
    mk = MK(nc, stage)
    with contextlib.ExitStack() as ctx:
        mk.declare(ctx)
        mk.emit()
        with nc.Block() as block:
            block.tensor(mk.T.replay)
            block.vector(mk.V.replay)
            block.scalar(mk.S.replay)
            block.gpsimd(mk.G.replay)
            block.sync(mk.Y.replay)
    return nc


def host_inputs(full, core):
    b, s = core // 4, core % 4
    x = np.ascontiguousarray(full["L0_attn_x"][b, s * TCORE:(s + 1) * TCORE, :])
    d = {"x": x}
    for n, key in (("wq", "L0_attn_Wq"), ("wk", "L0_attn_Wk"),
                   ("wv", "L0_attn_Wv"), ("wo", "L0_attn_Wo")):
        d[n] = full[key]
    d["rw"] = full["L0_ffn_router_w"]
    d["rb"] = full["L0_ffn_router_b"].reshape(E, 1)
    d["rng"] = np.ascontiguousarray(full["L0_ffn_rn_g"].reshape(KC, 128).T)
    d["rnb"] = np.ascontiguousarray(full["L0_ffn_rn_b"].reshape(KC, 128).T)
    d["lng"] = np.ascontiguousarray(full["L0_ffn_ln_g"].reshape(E, KC, 128)
                                    .transpose(2, 0, 1))
    d["lnb"] = np.ascontiguousarray(full["L0_ffn_ln_b"].reshape(E, KC, 128)
                                    .transpose(2, 0, 1))
    d["w1"] = full["L0_ffn_w1"]
    d["b1"] = np.ascontiguousarray(full["L0_ffn_b1"].reshape(E, NIT, 128)
                                   .transpose(2, 0, 1))
    d["w2"] = full["L0_ffn_w2"]
    d["b2"] = full["L0_ffn_b2"]
    pos = np.arange(s * TCORE, (s + 1) * TCORE, dtype=np.float64)
    inv = 10000.0 ** (-np.arange(32, dtype=np.float64) / 32.0)
    ang = pos[:, None] * inv[None, :]
    c = np.cos(ang).astype(np.float32)
    sn = np.sin(ang).astype(np.float32)
    d["rc"] = np.tile(c, (1, 16)).astype(ml_dtypes.bfloat16)
    d["rs"] = np.tile(sn, (1, 16)).astype(ml_dtypes.bfloat16)
    d["msk"] = (np.arange(128)[:, None] <= np.arange(128)[None, :]).astype(np.float32)
    d["idb_h"] = np.eye(128, dtype=ml_dtypes.bfloat16)
    pm = np.zeros((E,), np.float32)
    for r in range(8):
        if r // 4 == b and r % 4 < s:
            pm[r] = 1.0
    d["pmk"] = np.tile(pm[None, :], (128, 1))
    return d




def kernel(**inputs):
    full = {k: np.asarray(v) for k, v in inputs.items()}
    nc = build_nc(stage=3)
    in_maps = [host_inputs(full, c) for c in range(8)]
    from concourse.bass_utils import run_bass_kernel_spmd
    res = run_bass_kernel_spmd(nc, in_maps, core_ids=list(range(8)))
    out = np.zeros((2, 16384, 1024), np.float32)
    for c in range(8):
        b, s = c // 4, c % 4
        out[b, s * TCORE:(s + 1) * TCORE] = np.asarray(res.results[c]["out"])
    return out


# version 05
# speedup vs baseline: 1.0008x; 1.0008x over previous
import numpy as np
import ml_dtypes
import contextlib
import concourse.bass as bass
import concourse.mybir as mybir

F32 = mybir.dt.float32
BF16 = mybir.dt.bfloat16
AF = mybir.ActivationFunctionType
ALU = mybir.AluOpType
AX = mybir.AxisListType

D = 1024; H = 16; KC = 8
I = 4096; E = 8; NIT = 32
TCORE = 4096; NTT = 4; TT = 1024
EPS = 1.0e-5; EPSA = 1.0e-6


class Ev:
    __slots__ = ("sem", "step", "n")
    def __init__(self, sem, step=1):
        self.sem = sem; self.step = step; self.n = 0
    def mark(self):
        return self.n


class Rec:
    def __init__(self, name):
        self.name = name; self.thunks = []; self.dma_ev = None
    def wait(self, ev, cnt=None):
        c = ev.n if cnt is None else cnt
        if c > 0:
            self.thunks.append(("w", ev.sem, c))
    def op(self, meth, *args, inc=None, **kw):
        if inc is None and meth == "dma_start" and self.dma_ev is not None:
            inc = self.dma_ev
        incs = []
        if inc is not None:
            inc.n += inc.step
            incs.append((inc.sem, inc.step))
        self.thunks.append(("o", meth, args, kw, incs))
    def replay(self, eng):
        for t in self.thunks:
            if t[0] == "w":
                eng.wait_ge(t[1], t[2])
            else:
                _, meth, args, kw, incs = t
                ins = getattr(eng, meth)(*args, **kw)
                for (sem, step) in incs:
                    ins.then_inc(sem, step)


class MK:
    def __init__(self, nc, stage):
        self.nc = nc
        self.stage = stage

    def declare(self, ctx):
        nc = self.nc
        dt = nc.dram_tensor
        ecm = ctx.enter_context
        self.x_h = dt("x", [TCORE, D], F32, kind="ExternalInput").ap()
        self.w_h = [dt(n, [D, D], F32, kind="ExternalInput").ap()
                    for n in ("wq", "wk", "wv", "wo")]
        self.rw_h = dt("rw", [E, D], F32, kind="ExternalInput").ap()
        self.rb_h = dt("rb", [E, 1], F32, kind="ExternalInput").ap()
        self.rng_h = dt("rng", [128, KC], F32, kind="ExternalInput").ap()
        self.rnb_h = dt("rnb", [128, KC], F32, kind="ExternalInput").ap()
        self.lng_h = dt("lng", [128, E, KC], F32, kind="ExternalInput").ap()
        self.lnb_h = dt("lnb", [128, E, KC], F32, kind="ExternalInput").ap()
        self.w1_h = dt("w1", [E, D, I], F32, kind="ExternalInput").ap()
        self.b1_h = dt("b1", [128, E, NIT], F32, kind="ExternalInput").ap()
        self.w2_h = dt("w2", [E, I, D], F32, kind="ExternalInput").ap()
        self.b2_h = dt("b2", [E, D], F32, kind="ExternalInput").ap()
        self.rc_h = dt("rc", [TCORE, 512], BF16, kind="ExternalInput").ap()
        self.rs_h = dt("rs", [TCORE, 512], BF16, kind="ExternalInput").ap()
        self.msk_h = dt("msk", [128, 128], F32, kind="ExternalInput").ap()
        self.idb_h = dt("idb_h", [128, 128], BF16, kind="ExternalInput").ap()
        self.pmk_h = dt("pmk", [128, E], F32, kind="ExternalInput").ap()
        self.out_h = dt("out", [TCORE, D], F32, kind="ExternalOutput").ap()
        if self.stage <= 2:
            self.dz_h = dt("d_z", [NTT, 128, KC, TT], BF16, kind="ExternalOutput").ap()
            self.dz2_h = dt("d_z2", [NTT, 128, KC, TT], BF16, kind="ExternalOutput").ap()
            self.dst_h = dt("d_st", [128, 8, 65], F32, kind="ExternalOutput").ap()
            self.dcc_h = dt("d_cc", [512, H * 65], F32, kind="ExternalOutput").ap()
        if self.stage == 2:
            self.dq_h = dt("d_qf", [128, 8, 512], BF16, kind="ExternalOutput").ap()
            self.dk_h = dt("d_kf", [128, 8, 512], BF16, kind="ExternalOutput").ap()
            self.dv_h = dt("d_vt", [128, 4, H * 65], BF16, kind="ExternalOutput").ap()
            self.dy_h = dt("d_yt", [128, 4, D], BF16, kind="ExternalOutput").ap()
            self.dc_h = dt("d_cmb", [128, KC, E], F32, kind="ExternalOutput").ap()

        self.s_wT = dt("s_wT", [4, 128, KC, D], BF16).ap()
        self.s_zfm = dt("s_zfm", [NTT, 128, KC, TT], BF16).ap()
        self.s_z2fm = dt("s_z2fm", [NTT, 128, KC, TT], BF16).ap()
        self.bw1_h = dt("bw1", [E, NIT, 128, D], BF16).ap()
        self.bw2_h = dt("bw2", [E, NIT, 128, D], BF16).ap()
        self.cc_in = dt("cc_in", [64, H * 65], F32).ap()
        self.cc_out = dt("cc_out", [8 * 64, H * 65], F32, addr_space="Shared").ap()

        sb = lambda n, s, d: ecm(nc.sbuf_tensor(n, s, d))
        self.zbuf = sb("zbuf", [128, KC, TT], BF16)
        self.hid = sb("hid", [128, NIT, 512], BF16)
        self.acc = sb("acc", [128, KC, D], F32)
        self.xs = sb("xs", [128, 2, D], F32)
        self.qf = sb("qf", [128, 8, 512], BF16)
        self.kf = sb("kf", [128, 8, 512], BF16)
        self.kt = sb("kt", [128, 4, D], BF16)
        self.vt = sb("vt", [128, 4, H * 65], BF16)
        self.yt = sb("yt", [128, 4, D], BF16)
        self.mstb = sb("mstb", [128, 2, 128], BF16)
        self.w1s = sb("w1s", [128, 4, KC, 128], BF16)
        self.w2s = sb("w2s", [128, 4, D], BF16)
        self.wts = sb("wts", [128, 2, KC, 512], BF16)
        self.rfc = sb("rfc", [128, 4, 512], BF16)
        self.rfs = sb("rfs", [128, 4, 512], BF16)
        self.scb = sb("scb", [128, 3, 512], BF16)
        self.sc3 = sb("sc3", [128, 2, 512], F32)
        self.z2e = sb("z2e", [128, KC, 512], BF16)
        self.st32 = sb("st32", [128, 8, 65], F32)
        self.stbf = sb("stbf", [128, 8, 65], BF16)

        self.cmbs = sb("cmbs", [128, KC, E], F32)
        self.cfm = sb("cfm", [8, KC, 128], BF16)
        self.mcol = sb("mcol", [128, 16], F32)
        self.emask = sb("emask", [128, 4, E], F32)
        self.idb = sb("idb", [128, 128], BF16)
        self.mskb = sb("mskb", [128, 128], F32)
        self.b1s = sb("b1s", [128, E, NIT], F32)
        self.lngs = sb("lngs", [128, E, KC], F32)
        self.lnbs = sb("lnbs", [128, E, KC], F32)
        self.b2s = sb("b2s", [E, D], BF16)
        self.rwT = sb("rwT", [128, KC, E], BF16)
        self.rngs = sb("rngs", [128, KC], F32)
        self.rnbs = sb("rnbs", [128, KC], F32)
        self.rnbb = sb("rnbb", [128, KC], BF16)
        self.cstc = sb("cstc", [E, 2], F32)
        self.cstbc = sb("cstbc", [E, 1], BF16)
        self.crow = sb("crow", [1, E], BF16)
        self.onesb = sb("onesb", [1, 128], BF16)
        self.pmks = sb("pmks", [128, E], F32)
        self.ps = ecm(nc.psum_tensor("ps", [128, 8, 512], F32))
        self.rwsb = self.z2e[0:8, 0:2, :].rearrange("p a b -> p (a b)")
        accf = self.acc[:].rearrange("p a b -> p (a b)")
        self.srb = accf[0:64, 0:2080].rearrange("p (a b) -> p a b", a=2)
        self.pstt = accf[0:64, 2080:2080 + H * 65]

        self.sem = {n: ecm(nc.semaphore(n)) for n in
                    ("sq", "gq", "pe", "dv", "ac", "cc", "aq", "vq")}
        self.T = Rec("t"); self.V = Rec("v"); self.S = Rec("s")
        self.G = Rec("g"); self.Y = Rec("y")
        self.e_sq = Ev(self.sem["sq"], 16); self.e_gq = Ev(self.sem["gq"], 16)
        self.Y.dma_ev = self.e_sq; self.G.dma_ev = self.e_gq
        self.e_pe = Ev(self.sem["pe"]); self.e_dv = Ev(self.sem["dv"])
        self.e_ac = Ev(self.sem["ac"]); self.e_cc = Ev(self.sem["cc"])
        self.e_aq = Ev(self.sem["aq"], 16)
        self.wq_rot = [(self.G, self.e_gq), (self.Y, self.e_sq)]

        self.xs_mark = [(0, 0), (0, 0)]
        self.yt_mark = [0, 0, 0, 0]
        self.tb_mark = [0, 0]
        self.pb_mark = [0, 0]
        self.stb_mark = [0, 0]
        self.ob_mark = [0, 0]
        self.sb_mark = [0, 0]
        self.mst_mark = [0, 0]
        self.wts_mark = [0, 0]
        self.w1s_mark = [0, 0, 0, 0]
        self.w2s_mark = [0, 0, 0, 0]
        self.scr_mark = [0]
        self.rf_mark = [0]
        self.zspill_mark = [0]
        self.zfm_ready = [0]
        self.z2_ready = [0]
        self.acc_store = [0]
        self.proj_bank = [0]
        self.proj_slot = [0]
        self.fb_mark = [0, 0]
        self.w2b_mark = [0] * 8
        self.pc_store_mark = [0, 0]
        self.pc_idx = 0
        self.pc_chunks = [(k, e, j0) for e in range(E) for k in range(2)
                          for j0 in range(0, NIT, 4)]
        self.precast_done = 0
        self.rt_mark = [0]
        self.cf_mark = [0]
        self.b2_mark = [0]
        self.z2e_mark = [0]
        self.z_used = [0]
        self.wo_done = [0]
        self.hid_free = 0
        self.in_p2 = False

    def mmop(self, out, lhsT, rhs, start=True, stop=True, inc=None):
        self.T.op("matmul", out, lhsT, rhs, start=start, stop=stop, inc=inc)

    def trans(self, out_ps, in_sb, k=128, inc=None):
        self.T.op("matmul", out_ps, in_sb, self.idb[0:k, 0:k], start=True,
                  stop=True, inc=inc)

    def vsep(self):
        self.V.wait(self.e_dv, self.e_dv.mark())

    def ssep(self):
        self.S.wait(self.e_ac, self.e_ac.mark())

    def bank_fence(self):
        self.T.wait(self.e_dv, self.e_dv.mark())
        self.T.wait(self.e_ac, self.e_ac.mark())

    def phase0(self):
        T, V, S, G, Y = self.T, self.V, self.S, self.G, self.Y
        ps, idb = self.ps, self.idb
        for (dstt, src) in ((idb[:], self.idb_h), (self.mskb[:], self.msk_h),
                            (self.b1s[:], self.b1_h), (self.lngs[:], self.lng_h),
                            (self.lnbs[:], self.lnb_h), (self.rngs[:], self.rng_h),
                            (self.rnbs[:], self.rnb_h), (self.pmks[:], self.pmk_h),
                            (self.cstc[:, 0:1], self.rb_h)):
            Y.op("dma_start", out=dstt, in_=src, inc=self.e_sq)
        G.op("dma_start", out=self.b2s[:], in_=self.b2_h, inc=self.e_gq)
        G.op("dma_start", out=self.rwsb, in_=self.rw_h, inc=self.e_gq)
        m_ld0 = self.e_sq.mark()

        V.wait(self.e_sq, m_ld0)
        V.op("tensor_copy", self.rnbb[:], self.rnbs[:])
        V.op("memset", self.onesb[:], 1.0)
        ones65 = self.vt[:].rearrange("p a (h q) -> p a h q", q=65)[:, :, :, 64:65]
        V.op("memset", ones65, 1.0)
        V.op("memset", self.mcol[:, 15:16], EPS)
        V.op("memset", self.st32[:], 0.0, inc=self.e_dv)
        m_init = self.e_dv.mark()

        hidf = self.hid[:].rearrange("p a b -> p (a b)")
        wst = hidf[:, 0:8192].rearrange("p (kc j) -> p kc j", kc=KC)
        m_wst_free = 0
        for m in range(4):
            G.wait(self.e_pe, self.e_pe.mark())
            G.op("dma_start", out=self.zbuf[:],
                 in_=self.w_h[m].rearrange("(jc p) d -> p jc d", p=128), inc=self.e_gq)
            T.wait(self.e_gq, self.e_gq.mark())
            T.wait(self.e_sq, m_ld0)
            T.wait(self.e_sq, m_wst_free)
            cnt = 0
            for jc in range(8):
                for kc in range(KC):
                    b = cnt % 2; cnt += 1
                    T.wait(self.e_dv, self.tb_mark[b])
                    self.trans(ps[:, b, 0:128],
                               self.zbuf[:, jc, 128 * kc:128 * (kc + 1)], inc=self.e_pe)
                    V.wait(self.e_pe, self.e_pe.mark())
                    V.op("tensor_copy", wst[:, kc, 128 * jc:128 * (jc + 1)],
                         ps[:, b, 0:128], inc=self.e_dv)
                    self.tb_mark[b] = self.e_dv.mark()
            Y.wait(self.e_dv, self.e_dv.mark())
            Y.op("dma_start", out=self.s_wT[m], in_=wst, inc=self.e_sq)
            m_wst_free = self.e_sq.mark()
        self.hid_free = m_wst_free

        T.wait(self.e_gq, self.e_gq.mark())
        T.wait(self.e_dv, m_init)
        for kc in range(KC):
            b = 4 + (kc % 2)
            if kc >= 2:
                T.wait(self.e_dv, self.e_dv.mark() - 2)
            self.trans(ps[:, b, 0:8], self.rwsb[:, 128 * kc:128 * (kc + 1)], k=8,
                       inc=self.e_pe)
            V.wait(self.e_pe, self.e_pe.mark())
            V.op("tensor_scalar", self.rwT[:, kc, :], ps[:, b, 0:8],
                 self.rngs[:, kc:kc + 1], None, ALU.mult)
            V.op("tensor_copy", self.scb[:, 0, 8 * kc:8 * (kc + 1)], ps[:, b, 0:8],
                 inc=self.e_dv)
        T.wait(self.e_dv, self.e_dv.mark())
        for kc in range(KC):
            self.mmop(ps[0:8, 6, 0:1], self.scb[:, 0, 8 * kc:8 * (kc + 1)],
                      self.rnbb[:, kc:kc + 1], start=(kc == 0), stop=(kc == KC - 1),
                      inc=(self.e_pe if kc == KC - 1 else None))
        V.wait(self.e_pe, self.e_pe.mark())
        V.wait(self.e_sq, m_ld0)
        V.op("tensor_tensor", self.cstc[:, 0:1], self.cstc[:, 0:1], ps[0:8, 6, 0:1],
             ALU.add, inc=self.e_dv)
        self.vsep()
        V.op("tensor_copy", self.cstbc[:], self.cstc[:, 0:1], inc=self.e_dv)
        T.wait(self.e_dv, self.e_dv.mark())
        self.trans(ps[0:1, 7, 0:8], self.cstbc[:], k=8, inc=self.e_pe)
        V.wait(self.e_pe, self.e_pe.mark())
        V.op("tensor_copy", self.crow[:], ps[0:1, 7, 0:8], inc=self.e_dv)

    def ln_stats(self, src):
        V, S = self.V, self.S
        mc = self.mcol
        sc3f = self.sc3[:].rearrange("p a b -> p (a b)")
        V.op("reduce_sum", mc[:, 0:1], src, AX.X, inc=self.e_dv)
        S.wait(self.e_dv, self.e_dv.mark())
        S.op("square", sc3f, src, inc=self.e_ac)
        V.wait(self.e_ac, self.e_ac.mark())
        V.op("reduce_sum", mc[:, 1:2], sc3f, AX.X, inc=self.e_dv)
        S.wait(self.e_dv, self.e_dv.mark())
        S.op("activation", mc[:, 2:3], mc[:, 0:1], AF.Identity, scale=1.0 / D)
        S.op("activation", mc[:, 3:4], mc[:, 1:2], AF.Identity, scale=1.0 / D,
             inc=self.e_ac)
        V.wait(self.e_ac, self.e_ac.mark())
        V.op("scalar_tensor_tensor", mc[:, 4:5], mc[:, 2:3], mc[:, 2:3],
             mc[:, 3:4], ALU.mult, ALU.subtract, inc=self.e_dv)
        S.wait(self.e_dv, self.e_dv.mark())
        S.op("activation", mc[:, 6:7], mc[:, 4:5], AF.Sqrt, bias=mc[:, 15:16],
             scale=-1.0, inc=self.e_ac)
        V.wait(self.e_ac, self.e_ac.mark())
        V.op("reciprocal", mc[:, 7:8], mc[:, 6:7], inc=self.e_dv)
        self.vsep()
        V.op("scalar_tensor_tensor", mc[:, 8:9], mc[:, 2:3], -1.0, mc[:, 7:8],
             ALU.mult, ALU.mult, inc=self.e_dv)

    def tok_to_fm(self, src_tok, dst_fm, sub):
        T, V = self.T, self.V
        ps = self.ps
        for c2 in range(4):
            b = 2 + (c2 % 2)
            T.wait(self.e_dv, self.tb_mark[c2 % 2])
            self.trans(ps[:, b, 0:128], src_tok[:, 256 * c2:256 * c2 + 128])
            self.trans(ps[:, b, 128:256], src_tok[:, 256 * c2 + 128:256 * (c2 + 1)],
                       inc=self.e_pe)
            V.wait(self.e_pe, self.e_pe.mark())
            V.op("tensor_copy",
                 dst_fm[:, 2 * c2:2 * c2 + 2, 128 * sub:128 * (sub + 1)],
                 ps[:, b, 0:256].rearrange("p (a q) -> p a q", a=2), inc=self.e_dv)
            self.tb_mark[c2 % 2] = self.e_dv.mark()

    def ln_subtile(self, tt, sub):
        T, V, S, Y = self.T, self.V, self.S, self.Y
        slot = sub % 2
        ztk = self.yt[:, slot, :]
        z2tk = self.yt[:, 2 + slot, :]
        Y.wait(self.e_dv, self.xs_mark[slot][0])
        Y.wait(self.e_ac, self.xs_mark[slot][1])
        Y.op("dma_start", out=self.xs[:, slot],
             in_=self.x_h[(tt * 8 + sub) * 128:(tt * 8 + sub + 1) * 128, :],
             inc=self.e_sq)
        m_x = self.e_sq.mark()
        V.wait(self.e_sq, m_x)
        S.wait(self.e_sq, m_x)
        self.ln_stats(self.xs[:, slot])
        S.wait(self.e_dv, self.e_dv.mark())
        S.wait(self.e_pe, self.yt_mark[slot])
        S.op("activation", ztk, self.xs[:, slot], AF.Identity,
             bias=self.mcol[:, 8:9], scale=self.mcol[:, 7:8], inc=self.e_ac)
        m_z = self.e_ac.mark()
        self.xs_mark[slot] = (self.e_dv.mark(), self.e_ac.mark())
        V.wait(self.e_ac, m_z)
        self.ln_stats(ztk)
        S.wait(self.e_dv, self.e_dv.mark())
        S.wait(self.e_pe, self.yt_mark[2 + slot])
        S.op("activation", z2tk, ztk, AF.Identity,
             bias=self.mcol[:, 8:9], scale=self.mcol[:, 7:8], inc=self.e_ac)
        m_z2 = self.e_ac.mark()
        hidf = self.hid[:].rearrange("p a b -> p (a b)")
        z2st = hidf[:, 0:8192].rearrange("p (c t) -> p c t", c=KC)
        T.wait(self.e_ac, m_z)
        T.wait(self.e_sq, self.zspill_mark[0])
        if tt == 0 and sub == 0:
            T.wait(self.e_sq, self.hid_free)
        self.tok_to_fm(ztk, self.zbuf, sub)
        self.yt_mark[slot] = self.e_pe.mark()
        T.wait(self.e_ac, m_z2)
        self.tok_to_fm(z2tk, z2st, sub)
        self.yt_mark[2 + slot] = self.e_pe.mark()

    def precast_step(self):
        if self.pc_idx >= len(self.pc_chunks):
            return
        kind, e, j0 = self.pc_chunks[self.pc_idx]
        b = self.pc_idx % 2; self.pc_idx += 1
        G, Y = self.G, self.Y
        stg = self.hid[:, 16 + 8 * b:24 + 8 * b, :] \
            .rearrange("p a c -> p (a c)").rearrange("p (j t) -> p j t", j=4)
        G.wait(self.e_sq, self.pc_store_mark[b])
        for j in range(4):
            if kind == 0:
                src = self.w1_h[e][:, 128 * (j0 + j):128 * (j0 + j + 1)] \
                    .rearrange("(c p) i -> p c i", p=128)
                dstg = stg[:, j].rearrange("p (c i) -> p c i", c=KC)
            else:
                src = self.w2_h[e][128 * (j0 + j):128 * (j0 + j + 1), :]
                dstg = stg[:, j]
            G.op("dma_start", out=dstg, in_=src)
        m_ld = self.e_gq.mark()
        Y.wait(self.e_gq, m_ld)
        dst = (self.bw1_h if kind == 0 else self.bw2_h)[e, j0:j0 + 4] \
            .rearrange("a p c -> p a c")
        Y.op("dma_start", out=dst, in_=stg)
        self.pc_store_mark[b] = self.e_sq.mark()

    def load_fields(self, tt, hh):
        Y = self.Y
        Y.wait(self.e_dv, self.rf_mark[0])
        base = tt * TT + 512 * hh
        Y.op("dma_start", out=self.rfc[:],
             in_=self.rc_h[base:base + 512, :].rearrange("(a p) f -> p a f", p=128))
        Y.op("dma_start", out=self.rfs[:],
             in_=self.rs_h[base:base + 512, :].rearrange("(a p) f -> p a f", p=128),
             inc=self.e_sq)
        return self.e_sq.mark()

    def rope_phi(self, tc4, seg, psb, dest, m_field, trans_to):
        T, V, S = self.T, self.V, self.S
        ps = self.ps
        s0, s1, s2 = self.scb[:, 0], self.scb[:, 1], self.scb[:, 2]
        rs = lambda t: t.rearrange("p (h i) -> p h i", i=64)
        A0, B0 = rs(s0)[:, :, 0:32], rs(s0)[:, :, 32:64]
        A1, B1 = rs(s1)[:, :, 0:32], rs(s1)[:, :, 32:64]
        A2, B2 = rs(s2)[:, :, 0:32], rs(s2)[:, :, 32:64]
        FA = self.rfc[:, tc4].rearrange("p (h i) -> p h i", i=32)[:, 0:8, :]
        FS = self.rfs[:, tc4].rearrange("p (h i) -> p h i", i=32)[:, 0:8, :]
        V.wait(self.e_sq, m_field)
        V.wait(self.e_pe, self.scr_mark[0])
        V.op("tensor_copy", s0, ps[:, psb, :], inc=self.e_dv)
        m_psum_free = self.e_dv.mark()
        self.vsep()
        V.op("tensor_tensor", A1, A0, FA, ALU.mult)
        V.op("tensor_tensor", A2, A0, FS, ALU.mult)
        V.op("tensor_tensor", B2, B0, FA, ALU.mult)
        V.op("tensor_tensor", B1, B0, FS, ALU.mult)
        V.op("tensor_tensor", A1, A1, B1, ALU.subtract)
        V.thunks.pop()
        V.op("tensor_tensor", B2, A2, B2, ALU.add, inc=self.e_dv)
        self.vsep()
        V.op("tensor_tensor", A1, A1, B1, ALU.subtract)
        V.op("tensor_copy", B1, B2, inc=self.e_dv)
        self.vsep()
        V.op("tensor_scalar", self.sc3[:, 0], s1, 0.0, None, ALU.min, inc=self.e_dv)
        S.wait(self.e_dv, self.e_dv.mark())
        S.op("activation", self.sc3[:, 1], self.sc3[:, 0], AF.Exp, inc=self.e_ac)
        V.wait(self.e_ac, self.e_ac.mark())
        V.op("scalar_tensor_tensor", dest, s1, 0.0, self.sc3[:, 1],
             ALU.max, ALU.add, inc=self.e_dv)
        if trans_to is not None:
            m_dest = self.e_dv.mark()
            T.wait(self.e_dv, m_dest)
            b = 2 + (seg % 2)
            T.wait(self.e_dv, self.tb_mark[seg % 2])
            for jb in range(4):
                self.trans(ps[:, b, 128 * jb:128 * (jb + 1)],
                           dest[:, 128 * jb:128 * (jb + 1)],
                           inc=(self.e_pe if jb == 3 else None))
            self.scr_mark[0] = self.e_pe.mark()
            V.wait(self.e_pe, self.e_pe.mark())
            V.op("tensor_copy",
                 trans_to[:, 4 * seg:4 * seg + 4, 128 * tc4:128 * (tc4 + 1)],
                 ps[:, b, :].rearrange("p (a q) -> p a q", a=4), inc=self.e_dv)
            self.tb_mark[seg % 2] = self.e_dv.mark()
        return m_psum_free

    def proj_half(self, hh, which, m_idx, m_field):
        T, V, Y = self.T, self.V, self.Y
        ps = self.ps
        V.wait(self.e_pe, self.e_pe.mark())
        for seg in range(2):
            slot = self.proj_slot[0] % 2; self.proj_slot[0] += 1
            Y.wait(self.e_pe, self.wts_mark[slot])
            Y.op("dma_start", out=self.wts[:, slot],
                 in_=self.s_wT[m_idx][:, :, 512 * seg:512 * (seg + 1)], inc=self.e_sq)
            m_w = self.e_sq.mark()
            for tc in range(4):
                tcg = 4 * hh + tc
                b = self.proj_bank[0] % 2; self.proj_bank[0] += 1
                T.wait(self.e_sq, m_w)
                T.wait(self.e_dv, self.pb_mark[b])
                T.wait(self.e_dv, self.zfm_ready[0])
                for kc in range(KC):
                    self.mmop(ps[:, b, :], self.zbuf[:, kc, 128 * tcg:128 * (tcg + 1)],
                              self.wts[:, slot, kc, :], start=(kc == 0),
                              stop=(kc == KC - 1),
                              inc=(self.e_pe if kc == KC - 1 else None))
                if tc == 3:
                    self.wts_mark[slot] = self.e_pe.mark()
                V.wait(self.e_pe, self.e_pe.mark())
                if which == "v":
                    vvt = self.vt[:, tc].rearrange("p (h q) -> p h q", q=65)
                    V.op("tensor_copy", vvt[:, 8 * seg:8 * seg + 8, 0:64],
                         ps[:, b, :].rearrange("p (h e) -> p h e", e=64),
                         inc=self.e_dv)
                    self.pb_mark[b] = self.e_dv.mark()
                else:
                    if which == "k":
                        dest = self.kt[:, tc, 512 * seg:512 * (seg + 1)]
                        tr = self.kf if self.in_p2 else None
                    else:
                        dest = self.scb[:, 1]
                        tr = self.qf
                    self.pb_mark[b] = self.rope_phi(tc, seg, b, dest, m_field, tr)

    def state_chunk(self, tc, with_stbf):
        T, V = self.T, self.V
        ps = self.ps
        T.wait(self.e_dv, self.e_dv.mark())
        for pr in range(8):
            b = 6 + (pr % 2)
            T.wait(self.e_dv, self.stb_mark[pr % 2])
            self.mmop(ps[0:64, b, 0:65], self.kt[:, tc, 128 * pr:128 * pr + 64],
                      self.vt[:, tc, 65 * 2 * pr:65 * 2 * pr + 65])
            self.mmop(ps[64:128, b, 0:65], self.kt[:, tc, 128 * pr + 64:128 * (pr + 1)],
                      self.vt[:, tc, 65 * (2 * pr + 1):65 * (2 * pr + 1) + 65],
                      inc=self.e_pe)
            V.wait(self.e_pe, self.e_pe.mark())
            V.op("tensor_tensor", self.st32[:, pr], self.st32[:, pr], ps[:, b, 0:65],
                 ALU.add, inc=self.e_dv)
            if with_stbf:
                self.vsep()
                V.op("tensor_copy", self.stbf[:, pr], self.st32[:, pr], inc=self.e_dv)
            self.stb_mark[pr % 2] = self.e_dv.mark()

    def phase1(self):
        self.in_p2 = False
        for tt in range(NTT):
            for sub in range(8):
                self.ln_subtile(tt, sub)
                self.precast_step()
                self.precast_step()
            self.zfm_ready[0] = self.e_dv.mark()
            for hh in range(2):
                m_field = self.load_fields(tt, hh)
                self.proj_half(hh, "k", 1, m_field)
                self.proj_half(hh, "v", 2, m_field)
                self.bank_fence()
                for tc in range(4):
                    self.state_chunk(tc, with_stbf=False)
                self.rf_mark[0] = self.e_dv.mark()
            Y = self.Y
            Y.wait(self.e_dv, self.e_dv.mark())
            Y.op("dma_start", out=self.s_zfm[tt], in_=self.zbuf[:])
            hidf = self.hid[:].rearrange("p a b -> p (a b)")
            z2st = hidf[:, 0:8192].rearrange("p (c t) -> p c t", c=KC)
            Y.op("dma_start", out=self.s_z2fm[tt], in_=z2st, inc=self.e_sq)
            self.zspill_mark[0] = self.e_sq.mark()

    def phase_cc(self):
        V, G, Y = self.V, self.G, self.Y
        cci = self.cc_in.rearrange("p (pr q) -> p pr q", q=130)
        Y.wait(self.e_dv, self.e_dv.mark())
        Y.op("dma_start", out=cci[:, :, 0:65], in_=self.st32[0:64, :, :])
        Y.op("dma_start", out=cci[:, :, 65:130], in_=self.st32[64:128, :, :],
             inc=self.e_sq)
        G.wait(self.e_sq, self.e_sq.mark())
        G.op("collective_compute", "AllGather", ALU.bypass,
             replica_groups=[list(range(8))], ins=[self.cc_in[:]],
             outs=[self.cc_out[:]], inc=self.e_cc)
        srm = [0, 0]
        for r in range(8):
            Y.wait(self.e_cc, 1)
            Y.wait(self.e_dv, srm[r % 2])
            Y.op("dma_start", out=self.srb[:, r % 2],
                 in_=self.cc_out[64 * r:64 * (r + 1), :], inc=self.e_sq)
            m_r = self.e_sq.mark()
            V.wait(self.e_sq, m_r)
            if r == 0:
                V.op("tensor_scalar", self.pstt, self.srb[:, 0],
                     self.pmks[0:64, 0:1], None, ALU.mult, inc=self.e_dv)
            else:
                self.vsep()
                V.op("scalar_tensor_tensor", self.pstt, self.srb[:, r % 2],
                     self.pmks[0:64, r:r + 1], self.pstt, ALU.mult, ALU.add,
                     inc=self.e_dv)
            srm[r % 2] = self.e_dv.mark()
        pstv = self.pstt.rearrange("p (pr q) -> p pr q", q=130)
        Y.wait(self.e_dv, self.e_dv.mark())
        Y.op("dma_start", out=self.st32[0:64, :, :], in_=pstv[:, :, 0:65])
        Y.op("dma_start", out=self.st32[64:128, :, :], in_=pstv[:, :, 65:130],
             inc=self.e_sq)
        V.wait(self.e_sq, self.e_sq.mark())
        V.op("tensor_copy", self.stbf[:], self.st32[:], inc=self.e_dv)

    def scan_chunk(self, tc):
        T, V = self.T, self.V
        ps = self.ps
        for grp in range(4):
            ob = 3 if grp % 2 == 0 else 5
            for j in range(4):
                h = 4 * grp + j
                pr = h // 2; od = h % 2
                sb = 2 if od == 0 else 4
                T.wait(self.e_dv, self.sb_mark[od])
                self.mmop(ps[:, sb, 0:128],
                          self.kf[64 * od:64 * (od + 1), pr, 128 * tc:128 * (tc + 1)],
                          self.qf[64 * od:64 * (od + 1), pr, 128 * tc:128 * (tc + 1)],
                          inc=self.e_pe)
                V.wait(self.e_pe, self.e_pe.mark())
                V.wait(self.e_pe, self.mst_mark[od])
                V.op("tensor_tensor", self.mstb[:, od], ps[:, sb, 0:128], self.mskb[:],
                     ALU.mult, inc=self.e_dv)
                self.sb_mark[od] = self.e_dv.mark()
                T.wait(self.e_dv, self.sb_mark[od])
                if j == 0:
                    T.wait(self.e_dv, self.ob_mark[grp % 2])
                self.mmop(ps[:, ob, 65 * j:65 * j + 65], self.mstb[:, od],
                          self.vt[:, tc, 65 * h:65 * h + 65], start=True, stop=False)
                self.mmop(ps[:, ob, 65 * j:65 * j + 65],
                          self.qf[64 * od:64 * (od + 1), pr, 128 * tc:128 * (tc + 1)],
                          self.stbf[64 * od:64 * (od + 1), pr, :],
                          start=False, stop=True, inc=self.e_pe)
                self.mst_mark[od] = self.e_pe.mark()
            m_o = self.e_pe.mark()
            V.wait(self.e_pe, m_o)
            dcols = ps[:, ob, 0:260].rearrange("p (j q) -> p j q", q=65)[:, :, 64:65] \
                .rearrange("p a b -> p (a b)")
            V.op("tensor_scalar", self.mcol[:, 9:13], dcols, EPSA, None, ALU.max,
                 inc=self.e_dv)
            self.vsep()
            V.op("reciprocal", self.mcol[:, 9:13], self.mcol[:, 9:13], inc=self.e_dv)
            self.vsep()
            for j in range(4):
                h = 4 * grp + j
                V.op("tensor_scalar", self.yt[:, tc, 64 * h:64 * (h + 1)],
                     ps[:, ob, 65 * j:65 * j + 64], self.mcol[:, 9 + j:10 + j],
                     None, ALU.mult, inc=self.e_dv)
            self.ob_mark[grp % 2] = self.e_dv.mark()
        self.state_chunk(tc, with_stbf=True)

    def wo_and_accinit(self, tt, hh):
        T, V, Y = self.T, self.V, self.Y
        ps = self.ps
        for tc in range(4):
            T.wait(self.e_dv, self.e_dv.mark())
            for jb2 in range(4):
                b = 2 + (jb2 % 2)
                T.wait(self.e_dv, self.tb_mark[jb2 % 2])
                self.trans(ps[:, b, 0:128],
                           self.yt[:, tc, 256 * jb2:256 * jb2 + 128])
                self.trans(ps[:, b, 128:256],
                           self.yt[:, tc, 256 * jb2 + 128:256 * (jb2 + 1)],
                           inc=self.e_pe)
                V.wait(self.e_pe, self.e_pe.mark())
                V.op("tensor_copy",
                     self.qf[:, 2 * jb2:2 * jb2 + 2, 128 * tc:128 * (tc + 1)],
                     ps[:, b, 0:256].rearrange("p (a q) -> p a q", a=2),
                     inc=self.e_dv)
                self.tb_mark[jb2 % 2] = self.e_dv.mark()
        m_yf = self.e_dv.mark()
        m_w = [0, 0]
        for seg in range(2):
            slot = self.proj_slot[0] % 2; self.proj_slot[0] += 1
            Y.wait(self.e_pe, self.wts_mark[slot])
            Y.op("dma_start", out=self.wts[:, slot],
                 in_=self.s_wT[3][:, :, 512 * seg:512 * (seg + 1)], inc=self.e_sq)
            m_w[slot] = self.e_sq.mark()
        for tc in range(4):
            tcg = 4 * hh + tc
            slot_x = tcg % 2
            Y.wait(self.e_dv, self.xs_mark[slot_x][0])
            Y.wait(self.e_ac, self.xs_mark[slot_x][1])
            Y.op("dma_start", out=self.xs[:, slot_x],
                 in_=self.x_h[(tt * 8 + tcg) * 128:(tt * 8 + tcg + 1) * 128, :],
                 inc=self.e_sq)
            m_x = self.e_sq.mark()
            for seg in range(2):
                slot = (self.proj_slot[0] - 2 + seg) % 2
                b = seg
                T.wait(self.e_sq, m_w[slot])
                T.wait(self.e_dv, m_yf)
                T.wait(self.e_dv, self.pb_mark[b])
                for pr in range(8):
                    self.mmop(ps[:, b, :], self.qf[:, pr, 128 * tc:128 * (tc + 1)],
                              self.wts[:, slot, pr, :], start=(pr == 0),
                              stop=(pr == 7), inc=(self.e_pe if pr == 7 else None))
                V.wait(self.e_pe, self.e_pe.mark())
                V.wait(self.e_sq, m_x)
                V.wait(self.e_sq, self.acc_store[0])
                V.op("tensor_tensor", self.acc[:, tcg, 512 * seg:512 * (seg + 1)],
                     ps[:, b, :], self.xs[:, slot_x, 512 * seg:512 * (seg + 1)],
                     ALU.add, inc=self.e_dv)
                self.pb_mark[b] = self.e_dv.mark()
            self.xs_mark[slot_x] = (self.e_dv.mark(), self.e_ac.mark())
        self.wts_mark[0] = self.e_pe.mark()
        self.wts_mark[1] = self.e_pe.mark()
        self.wo_done[0] = self.e_pe.mark()

    def router_tile(self, tcg):
        T, V, S = self.T, self.V, self.S
        ps = self.ps
        em = self.emask
        ltok, eq1, eq2, msked = em[:, 0], em[:, 1], em[:, 2], em[:, 3]
        mc = self.mcol
        T.wait(self.e_sq, self.z2_ready[0])
        T.wait(self.e_dv, self.rt_mark[0])
        for kc in range(KC):
            self.mmop(ps[:, 4, 0:8], self.zbuf[:, kc, 128 * tcg:128 * (tcg + 1)],
                      self.rwT[:, kc, :], start=(kc == 0), stop=False)
        self.mmop(ps[:, 4, 0:8], self.onesb[:], self.crow[:], start=False,
                  stop=True, inc=self.e_pe)
        V.wait(self.e_pe, self.e_pe.mark())
        V.op("tensor_copy", ltok, ps[:, 4, 0:8])
        V.op("reduce_max", mc[:, 9:10], ps[:, 4, 0:8], AX.X, inc=self.e_dv)
        self.rt_mark[0] = self.e_dv.mark()
        self.vsep()
        V.op("tensor_scalar", eq1, ltok, mc[:, 9:10], None, ALU.is_equal,
             inc=self.e_dv)
        self.vsep()
        V.op("scalar_tensor_tensor", msked, eq1, -1.0e30, ltok, ALU.mult, ALU.add,
             inc=self.e_dv)
        self.vsep()
        V.op("reduce_max", mc[:, 10:11], msked, AX.X, inc=self.e_dv)
        self.vsep()
        V.op("tensor_scalar", eq2, ltok, mc[:, 10:11], None, ALU.is_equal)
        V.op("tensor_tensor", mc[:, 11:12], mc[:, 10:11], mc[:, 9:10], ALU.subtract,
             inc=self.e_dv)
        S.wait(self.e_dv, self.e_dv.mark())
        S.op("activation", mc[:, 13:14], mc[:, 11:12], AF.Sigmoid, scale=-1.0)
        S.op("activation", mc[:, 14:15], mc[:, 11:12], AF.Sigmoid, inc=self.e_ac)
        V.wait(self.e_ac, self.e_ac.mark())
        V.op("tensor_scalar", eq1, eq1, mc[:, 13:14], None, ALU.mult, inc=self.e_dv)
        self.vsep()
        V.op("scalar_tensor_tensor", self.cmbs[:, tcg], eq2, mc[:, 14:15], eq1,
             ALU.mult, ALU.add, inc=self.e_dv)
        self.vsep()
        V.op("tensor_copy", self.scb[:, 0, 0:8], self.cmbs[:, tcg], inc=self.e_dv)
        m_cb = self.e_dv.mark()
        T.wait(self.e_dv, m_cb)
        T.wait(self.e_dv, self.cf_mark[0])
        self.trans(ps[0:8, 5, 0:128], self.scb[:, 0, 0:8], inc=self.e_pe)
        V.wait(self.e_pe, self.e_pe.mark())
        V.op("tensor_copy", self.cfm[:, tcg, :], ps[0:8, 5, 0:128], inc=self.e_dv)
        self.cf_mark[0] = self.e_dv.mark()

    def b2_term(self):
        T, V = self.T, self.V
        ps = self.ps
        for tcg in range(8):
            for seg in range(2):
                T.wait(self.e_dv, self.b2_mark[0])
                self.mmop(ps[:, 6, :], self.cfm[:, tcg, :],
                          self.b2s[:, 512 * seg:512 * (seg + 1)], inc=self.e_pe)
                V.wait(self.e_pe, self.e_pe.mark())
                V.wait(self.e_dv, 0)
                V.op("tensor_tensor", self.acc[:, tcg, 512 * seg:512 * (seg + 1)],
                     self.acc[:, tcg, 512 * seg:512 * (seg + 1)], ps[:, 6, :],
                     ALU.add, inc=self.e_dv)
                self.b2_mark[0] = self.e_dv.mark()

    def ffn_expert(self, tt, e):
        T, V, S, G = self.T, self.V, self.S, self.G
        ps = self.ps
        for hh in range(2):
            V.wait(self.e_sq, self.z2_ready[0])
            V.wait(self.e_pe, self.z2e_mark[0])
            for kc in range(KC):
                V.op("tensor_scalar", self.z2e[:, kc],
                     self.zbuf[:, kc, 512 * hh:512 * (hh + 1)],
                     self.lngs[:, e, kc:kc + 1], self.lnbs[:, e, kc:kc + 1],
                     ALU.mult, op1=ALU.add,
                     inc=(self.e_dv if kc == KC - 1 else None))
            m_z2e = self.e_dv.mark()
            for it in range(NIT):
                slot = it % 4
                qr, qe = self.wq_rot[it % 2]
                qr.wait(self.e_pe, self.w1s_mark[slot])
                qr.op("dma_start", out=self.w1s[:, slot],
                      in_=self.bw1_h[e, it].rearrange("p (c i) -> p c i", c=KC),
                      inc=qe)
                m_wl = qe.mark()
                fb = 6 + (it % 2)
                T.wait(qe, m_wl)
                T.wait(self.e_dv, m_z2e)
                T.wait(self.e_ac, self.fb_mark[it % 2])
                if it < 2:
                    T.wait(self.e_dv, self.w2b_mark[6 + it])
                for kc in range(KC):
                    self.mmop(ps[:, fb, :], self.w1s[:, slot, kc], self.z2e[:, kc],
                              start=(kc == 0), stop=(kc == KC - 1),
                              inc=(self.e_pe if kc == KC - 1 else None))
                self.w1s_mark[slot] = self.e_pe.mark()
                S.wait(self.e_pe, self.e_pe.mark())
                if tt == 0 and e == 0 and hh == 0 and it == 0:
                    S.wait(self.e_sq, self.zspill_mark[0])
                S.op("activation", self.hid[:, it], ps[:, fb, :], AF.Gelu,
                     bias=self.b1s[:, e, it:it + 1], inc=self.e_ac)
                self.fb_mark[it % 2] = self.e_ac.mark()
            self.z2e_mark[0] = self.e_pe.mark()
            m_gelu_all = self.e_ac.mark()
            for ic in range(NIT):
                slot = ic % 4
                qr, qe = self.wq_rot[(ic + 1) % 2]
                qr.wait(self.e_pe, self.w2s_mark[slot])
                qr.op("dma_start", out=self.w2s[:, slot],
                      in_=self.bw2_h[e, ic], inc=qe)
                m_wl = qe.mark()
                T.wait(qe, m_wl)
                if ic == 0:
                    T.wait(self.e_ac, m_gelu_all)
                    for bb in range(8):
                        T.wait(self.e_dv, self.w2b_mark[bb])
                for tc in range(4):
                    for seg in range(2):
                        bb = tc * 2 + seg
                        self.mmop(ps[:, bb, :],
                                  self.hid[:, ic, 128 * tc:128 * (tc + 1)],
                                  self.w2s[:, slot, 512 * seg:512 * (seg + 1)],
                                  start=(ic == 0), stop=(ic == NIT - 1),
                                  inc=(self.e_pe if (tc, seg) == (3, 1) else None))
                self.w2s_mark[slot] = self.e_pe.mark()
            m_w2 = self.e_pe.mark()
            V.wait(self.e_pe, m_w2)
            for bb in (6, 7, 0, 1, 2, 3, 4, 5):
                tc, seg = bb // 2, bb % 2
                tcg = 4 * hh + tc
                V.op("scalar_tensor_tensor",
                     self.acc[:, tcg, 512 * seg:512 * (seg + 1)],
                     ps[:, bb, :], self.cmbs[:, tcg, e:e + 1],
                     self.acc[:, tcg, 512 * seg:512 * (seg + 1)],
                     ALU.mult, ALU.add, inc=self.e_dv)
                self.w2b_mark[bb] = self.e_dv.mark()

    def phase2(self):
        self.in_p2 = True
        T, V, S, G, Y = self.T, self.V, self.S, self.G, self.Y
        G.wait(self.e_sq, self.precast_done)
        S.wait(self.e_sq, self.precast_done)
        for tt in range(NTT):
            Y.wait(self.e_pe, self.z_used[0])
            Y.wait(self.e_dv, self.zq_used_dv)
            Y.op("dma_start", out=self.zbuf[:], in_=self.s_zfm[tt], inc=self.e_sq)
            m_z = self.e_sq.mark()
            self.zfm_ready[0] = 0
            T.wait(self.e_sq, m_z)
            for hh in range(2):
                m_field = self.load_fields(tt, hh)
                self.bank_fence()
                self.proj_half(hh, "k", 1, m_field)
                self.proj_half(hh, "v", 2, m_field)
                self.scr_mark[0] = max(self.scr_mark[0], self.wo_done[0])
                self.proj_half(hh, "q", 0, m_field)
                if hh == 1:
                    self.z_used[0] = self.e_pe.mark()
                    Y.wait(self.e_pe, self.z_used[0])
                    Y.op("dma_start", out=self.zbuf[:], in_=self.s_z2fm[tt],
                         inc=self.e_sq)
                    self.z2_ready[0] = self.e_sq.mark()
                self.bank_fence()
                for tc in range(4):
                    self.scan_chunk(tc)
                self.bank_fence()
                self.wo_and_accinit(tt, hh)
                self.rf_mark[0] = self.e_dv.mark()
            self.bank_fence()
            if self.stage == 2 and tt == 0:
                self.dump_attn()
            for tcg in range(8):
                self.router_tile(tcg)
            self.b2_term()
            self.bank_fence()
            if self.stage >= 3:
                for e in range(E):
                    self.ffn_expert(tt, e)
            Y.wait(self.e_dv, self.e_dv.mark())
            Y.op("dma_start",
                 out=self.out_h[tt * TT:(tt + 1) * TT, :]
                 .rearrange("(tc p) d -> p tc d", p=128),
                 in_=self.acc[:], inc=self.e_sq)
            self.acc_store[0] = self.e_sq.mark()
            self.z_used[0] = self.e_pe.mark()
            self.zq_used_dv = self.e_dv.mark()
        Y.wait(self.e_sq, self.e_sq.mark())

    def dump_attn(self):
        Y = self.Y
        Y.wait(self.e_dv, self.e_dv.mark())
        for (dst, src) in ((self.dq_h, self.qf[:]), (self.dk_h, self.kf[:]),
                           (self.dv_h, self.vt[:]), (self.dy_h, self.yt[:])):
            Y.op("dma_start", out=dst, in_=src, inc=self.e_sq)

    def emit(self):
        self.zq_used_dv = 0
        self.phase0()
        self.phase1()
        while self.pc_idx < len(self.pc_chunks):
            self.precast_step()
        self.precast_done = self.e_sq.mark()
        self.phase_cc()
        if self.stage <= 2:
            Y = self.Y
            Y.op("dma_start", out=self.dcc_h, in_=self.cc_out, inc=self.e_sq)
            Y.op("dma_start", out=self.dz_h, in_=self.s_zfm, inc=self.e_sq)
            Y.op("dma_start", out=self.dz2_h, in_=self.s_z2fm, inc=self.e_sq)
            Y.wait(self.e_dv, self.e_dv.mark())
            Y.op("dma_start", out=self.dst_h, in_=self.st32[:], inc=self.e_sq)
        if self.stage == 1:
            V, Y = self.V, self.Y
            V.op("memset", self.acc[:], 0.0, inc=self.e_dv)
            Y.wait(self.e_dv, self.e_dv.mark())
            for tt in range(NTT):
                Y.op("dma_start",
                     out=self.out_h[tt * TT:(tt + 1) * TT, :]
                     .rearrange("(tc p) d -> p tc d", p=128),
                     in_=self.acc[:], inc=self.e_sq)
            Y.wait(self.e_sq, self.e_sq.mark())
        else:
            self.phase2()
        if self.stage == 2:
            Y = self.Y
            Y.wait(self.e_dv, self.e_dv.mark())
            Y.op("dma_start", out=self.dc_h, in_=self.cmbs[:], inc=self.e_sq)
            Y.wait(self.e_sq, self.e_sq.mark())


def build_nc(stage=3):
    nc = bass.Bass("TRN2", target_bir_lowering=False, debug=False)
    mk = MK(nc, stage)
    with contextlib.ExitStack() as ctx:
        mk.declare(ctx)
        mk.emit()
        with nc.Block() as block:
            block.tensor(mk.T.replay)
            block.vector(mk.V.replay)
            block.scalar(mk.S.replay)
            block.gpsimd(mk.G.replay)
            block.sync(mk.Y.replay)
    return nc


def host_inputs(full, core):
    b, s = core // 4, core % 4
    x = np.ascontiguousarray(full["L0_attn_x"][b, s * TCORE:(s + 1) * TCORE, :])
    d = {"x": x}
    for n, key in (("wq", "L0_attn_Wq"), ("wk", "L0_attn_Wk"),
                   ("wv", "L0_attn_Wv"), ("wo", "L0_attn_Wo")):
        d[n] = full[key]
    d["rw"] = full["L0_ffn_router_w"]
    d["rb"] = full["L0_ffn_router_b"].reshape(E, 1)
    d["rng"] = np.ascontiguousarray(full["L0_ffn_rn_g"].reshape(KC, 128).T)
    d["rnb"] = np.ascontiguousarray(full["L0_ffn_rn_b"].reshape(KC, 128).T)
    d["lng"] = np.ascontiguousarray(full["L0_ffn_ln_g"].reshape(E, KC, 128)
                                    .transpose(2, 0, 1))
    d["lnb"] = np.ascontiguousarray(full["L0_ffn_ln_b"].reshape(E, KC, 128)
                                    .transpose(2, 0, 1))
    d["w1"] = full["L0_ffn_w1"]
    d["b1"] = np.ascontiguousarray(full["L0_ffn_b1"].reshape(E, NIT, 128)
                                   .transpose(2, 0, 1))
    d["w2"] = full["L0_ffn_w2"]
    d["b2"] = full["L0_ffn_b2"]
    pos = np.arange(s * TCORE, (s + 1) * TCORE, dtype=np.float64)
    inv = 10000.0 ** (-np.arange(32, dtype=np.float64) / 32.0)
    ang = pos[:, None] * inv[None, :]
    c = np.cos(ang).astype(np.float32)
    sn = np.sin(ang).astype(np.float32)
    d["rc"] = np.tile(c, (1, 16)).astype(ml_dtypes.bfloat16)
    d["rs"] = np.tile(sn, (1, 16)).astype(ml_dtypes.bfloat16)
    d["msk"] = (np.arange(128)[:, None] <= np.arange(128)[None, :]).astype(np.float32)
    d["idb_h"] = np.eye(128, dtype=ml_dtypes.bfloat16)
    pm = np.zeros((E,), np.float32)
    for r in range(8):
        if r // 4 == b and r % 4 < s:
            pm[r] = 1.0
    d["pmk"] = np.tile(pm[None, :], (128, 1))
    return d




def kernel(**inputs):
    full = {k: np.asarray(v) for k, v in inputs.items()}
    nc = build_nc(stage=3)
    in_maps = [host_inputs(full, c) for c in range(8)]
    from concourse.bass_utils import run_bass_kernel_spmd
    res = run_bass_kernel_spmd(nc, in_maps, core_ids=list(range(8)))
    out = np.zeros((2, 16384, 1024), np.float32)
    for c in range(8):
        b, s = c // 4, c % 4
        out[b, s * TCORE:(s + 1) * TCORE] = np.asarray(res.results[c]["out"])
    return out
